# version 10
# baseline (speedup 1.0000x reference)
import contextlib
import numpy as np

import concourse.bass as bass
import concourse.mybir as mybir
from concourse.bass_utils import run_bass_kernel_spmd

F32 = mybir.dt.float32
BF16 = mybir.dt.bfloat16
FP16 = mybir.dt.float16
I32 = mybir.dt.int32
ALU = mybir.AluOpType
ACT = mybir.ActivationFunctionType
AXX = mybir.AxisListType.X

D = 2048
TC = 1024
NH = 16
NCHK = 8
MLP = 16384
NEG = -1.0e30
ENGS = ("sync", "tensor", "vector", "scalar", "gpsimd")


def build_program(stop_after=9, taps=(), dbg_ch=None, dbg_what="y"):
    nc = bass.Bass("TRN2", target_bir_lowering=False, detect_race_conditions=False)

    def din(name, shape, dt):
        return nc.dram_tensor(name, shape, dt, kind="ExternalInput").ap()

    xO = din("xO", [TC, D], F32)
    xF = din("xF", [TC, D], F32)
    flagF = din("flagF", [128, 1], F32)
    wq = din("wq", [D, D], F32)
    wk = din("wk", [D, D], F32)
    wv = din("wv", [D, D], F32)
    bq = din("bq", [D, 1], F32)
    bk = din("bk", [D, 1], F32)
    bv = din("bv", [D, 1], F32)
    wab = din("wab", [D, 64], F32)
    bab = din("bab", [64, 1], F32)
    gpk = din("gpk", [1024, MLP], I32)
    upk = din("upk", [1024, MLP], I32)
    dpk = din("dpk", [8192, D], I32)
    gsc = din("gsc", [1024, MLP], F32)
    usc = din("usc", [1024, MLP], F32)
    dsc = din("dsc", [8192, D], F32)
    ident = din("ident", [128, 128], F32)
    ones_row = din("ones_row", [1, 128], F32)
    ones_col = din("ones_col", [128, 1], F32)
    maskneg = din("maskneg", [128, 128], F32)
    sel16 = din("sel16", [16, NH * 128], F32)
    nsel16 = din("nsel16", [16, NH * 128], F32)
    zz128 = din("zz128", [128, 128], F32)
    constv = din("constv", [128, 3], F32)

    out = nc.dram_tensor("out", [TC, D], F32, kind="ExternalOutput").ap()
    tap_dram = {}
    for (tname, tshape, tdt) in taps:
        tap_dram[tname] = nc.dram_tensor("tap_" + tname, tshape, tdt, kind="ExternalOutput").ap()
    tap_sbuf = {}

    Q = {e: [] for e in ENGS}
    cnt = dict(sW=0, sG=0, sO=0, sP=0, sV=0, sA=0, sWm=0, sGm=0,
               sGt=0, sU=0, sD0=0, sD1=0, sD2=0, sD3=0, sWl=0, sGc=0)
    dma_sem = {"sync": "sW", "gpsimd": "sG"}
    ms_sem = {"sync": "sWm", "gpsimd": "sGm"}
    ms_covered = {"sync": 0, "gpsimd": 0}
    waited = {}

    def em(eng, fn):
        Q[eng].append(fn)

    def w8(eng, tok_):
        if tok_ is None:
            return
        if tok_[0] == "dma":
            _, queue, val = tok_
            if val > ms_covered[queue]:
                cur = cnt[dma_sem[queue]]
                em(queue, lambda e, s=dma_sem[queue], v=cur: e.wait_ge(SEMH[s], v))
                cnt[ms_sem[queue]] += 1
                em(queue, lambda e, s=ms_sem[queue]: e.sem_inc(SEMH[s], 1))
                ms_covered[queue] = cur
            tok_ = (ms_sem[queue], cnt[ms_sem[queue]])
        sem, val = tok_
        if val <= 0:
            return
        if waited.get((eng, sem), 0) >= val:
            return
        waited[(eng, sem)] = val
        em(eng, lambda e, s=sem, v=val: e.wait_ge(SEMH[s], v))

    def dma(eng, sem, out_ap, in_ap):
        sem = dma_sem.get(eng, sem) if sem in ("sW", "sG") else sem
        cnt[sem] += 16
        em(eng, lambda e, o=out_ap, i=in_ap, s=sem: e.dma_start(out=o, in_=i).then_inc(SEMH[s], 16))
        if sem in ("sO", "sGt", "sU", "sD0", "sD1", "sD2", "sD3", "sWl"):
            return (sem, cnt[sem])
        return ("dma", eng, cnt[sem])

    def op(eng, sem, fn):
        if sem is None:
            em(eng, fn)
            return None
        cnt[sem] += 1
        em(eng, lambda e, f=fn, s=sem: f(e).then_inc(SEMH[s], 1))
        return (sem, cnt[sem])

    def tk(sem):
        if sem == "sW":
            return ("dma", "sync", cnt["sW"])
        if sem == "sG":
            return ("dma", "gpsimd", cnt["sG"])
        return (sem, cnt[sem])

    with contextlib.ExitStack() as ctx:
        def sb(name, shape, dt, stack=None):
            return (stack or ctx).enter_context(nc.sbuf_tensor(name, shape, dt))

        SEMH = {s: ctx.enter_context(nc.semaphore(s)) for s in cnt}

        c_cv = sb("c_cv", [128, 3], F32)
        nc.const_aps.register(F32, 1e-5, c_cv[:, 0:1])
        nc.const_aps.register(F32, 1e-6, c_cv[:, 1:2])
        nc.const_aps.register(F32, 1.0, c_cv[:, 2:3])
        A_x = sb("A_x", [128, 8 * D], F32)
        xnT_hi = sb("xnT_hi", [128, 16 * TC], FP16)
        xnT_lo = sb("xnT_lo", [128, 16 * TC], FP16)

        PS = [ctx.enter_context(nc.psum_tensor(f"PS{i}", [128, 512], F32)) for i in range(4)]
        PD = [ctx.enter_context(nc.psum_tensor(f"PD{i}", [128, 1024], F32)) for i in range(2)]

        astk = ctx.enter_context(contextlib.ExitStack())
        c_id = sb("c_id", [128, 128], F32, astk)
        c_or = sb("c_or", [1, 128], F32, astk)
        c_oc = sb("c_oc", [128, 1], F32, astk)
        c_mask = sb("c_mask", [128, 128], F32, astk)
        c_zz = sb("c_zz", [16, 128], F32, astk)
        c_flag = sb("c_flag", [128, 1], F32, astk)
        c_bab = sb("c_bab", [64, 1], F32, astk)
        xn_s = sb("xn_s", [128, D], F32, astk)
        stat = sb("stat", [128, 16], F32, astk)
        wbuf0 = sb("wb0", [128, D], F32, astk)
        wbuf = [wbuf0, wbuf0]
        wsp_hi = [sb(f"wsph{i}", [128, D], FP16, astk) for i in range(2)]
        wsp_lo = [sb(f"wspl{i}", [128, D], FP16, astk) for i in range(2)]
        bcol = sb("bcol", [128, 4], F32, astk)
        qT = sb("qT", [128, TC], F32, astk)
        kT = sb("kT", [128, TC], F32, astk)
        vT = sb("vT", [128, TC], F32, astk)
        k_tok = sb("k_tok", [128, TC], F32, astk)
        v_tok = sb("v_tok", [128, TC], F32, astk)
        ksq = v_tok
        rn_row = sb("rn_row", [1, TC], F32, astk)
        rn_cols = sb("rn_cols", [128, NCHK], F32, astk)
        ab = sb("ab", [96, TC], F32, astk)
        cdF = sb("cdF", [16, TC], F32, astk)
        abT = sb("abT", [128, NCHK * 32], F32, astk)
        dteT = sb("dteT", [128, NCHK * 16], F32, astk)
        tdT = sb("tdT", [128, NCHK * 16], F32, astk)
        sdT = dteT
        kwt = sb("kwt", [128, 128], F32, astk)
        ce_row = rn_row
        dmt = sb("dmt", [128, 128], F32, astk)
        dket = dmt
        attnT = sb("attnT", [128, 128], F32, astk)
        q_cd = sb("q_cd", [128, 128], F32, astk)
        y_sb = sb("y_sb", [128, 128], F32, astk)
        state = sb("state", [128, NH * 128], F32, astk)
        brn_c = sb("brn_c", [128, 1], F32, astk)
        dbg_grab = c_mask if dbg_ch is not None else None
        c_sel = sb("c_sel", [16, 128], F32, astk)
        c_nsl = sb("c_nsl", [16, 128], F32, astk)

        t_const = None
        if stop_after < 1:
            t_dbgms = op("vector", "sV", lambda e: e.memset(A_x[:, :], 0.0))
            w8("sync", t_dbgms)
        for dst, src in ((c_id, ident), (c_or, ones_row), (c_oc, ones_col),
                         (c_mask, maskneg), (c_zz, zz128[0:16, :]), (c_flag, flagF), (c_bab, bab),
                         (c_cv, constv)):
            t_const = dma("sync", "sW", dst.ap(), src)
        for h in range(NH):
            t_const = dma("sync", "sW", state[:, h * 128:(h + 1) * 128], zz128)

        def emit_ln_transposes(x_dram, use_Ax, permuted):
            last_tr = [None]
            last_nrm = [None, None]
            t_last = [None]
            hi3 = xnT_hi.ap().rearrange("p (c f) -> p c f", f=TC)
            lo3 = xnT_lo.ap().rearrange("p (c f) -> p c f", f=TC)

            def do_transposes(t_n, bi, tc):
                if not permuted:
                    groups = [(bq_ * 4, [xn_s.ap()[:, (bq_ * 4 + q) * 128:(bq_ * 4 + q + 1) * 128]
                                         for q in range(4)]) for bq_ in range(4)]
                else:
                    groups = []
                    for par in range(2):
                        for bq_ in range(2):
                            srcs = [xn_s.ap()[:, (bq_ * 4 + q) * 256 + par:(bq_ * 4 + q) * 256 + 256:2]
                                    for q in range(4)]
                            groups.append((par * 8 + bq_ * 4, srcs))
                for gi, (kb, srcs) in enumerate(groups):
                    pt = PS[gi % 2]
                    w8("tensor", t_n)
                    w8("tensor", tk("sV"))
                    t_tr = None
                    for q4 in range(4):
                        t_tr = op("tensor", "sP" if q4 == 3 else None,
                                  lambda e, p=pt[:, q4 * 128:(q4 + 1) * 128], s_=srcs[q4]:
                                  e.transpose(p, s_, c_id[:, :]))
                    last_tr[0] = t_tr
                    w8("vector", t_tr)
                    hd = hi3[:, kb:kb + 4, tc * 128:(tc + 1) * 128]
                    ld_ = lo3[:, kb:kb + 4, tc * 128:(tc + 1) * 128]
                    op("vector", None, lambda e, p=pt, d=hd: e.tensor_copy(d, p[:, 0:512]))
                    t_last[0] = op("vector", "sV", lambda e, p=pt, d=ld_, h_=hd:
                                   e.scalar_tensor_tensor(d, p[:, 0:512], 1.0, h_, ALU.mult, ALU.subtract))

            if x_dram is not None and use_Ax:
                w8("sync", tk("sV"))
                w8("sync", tk("sA"))
            if x_dram is None:
                w8("scalar", tk("sV"))
                w8("vector", tk("sA"))
            for tc in range(NCHK):
                bi = tc % 2
                if use_Ax:
                    xt = A_x[:, tc * D:(tc + 1) * D]
                else:
                    xt = A_x[:, bi * D:(bi + 1) * D]
                    w8("sync", last_nrm[bi])
                t_x = None
                if x_dram is not None:
                    t_x = dma("sync", "sW", xt, x_dram[tc * 128:(tc + 1) * 128, :])
                sumc, sqc, muc = stat[:, 0:1], stat[:, 1:2], stat[:, 2:3]
                ex2c, mu2c, varc, rsc = stat[:, 3:4], stat[:, 4:5], stat[:, 5:6], stat[:, 6:7]
                w8("scalar", t_x)
                w8("scalar", last_tr[0])
                t_sq = op("scalar", "sA", lambda e, a=ksq[:, 0:1024], x_=xt[:, 0:1024], c=sqc:
                          e.activation(a, x_, ACT.Square, accum_out=c))
                t_sq = op("scalar", "sA", lambda e, a=ksq[:, 0:1024], x_=xt[:, 1024:2048], c=stat[:, 8:9]:
                          e.activation(a, x_, ACT.Square, accum_out=c))
                w8("vector", t_x)
                def vf(f):
                    w8("vector", op("vector", "sV", f))
                vf(lambda e, x_=xt, c=sumc: e.reduce_sum(c, x_, AXX))
                vf(lambda e, c=sumc, m=muc: e.tensor_scalar(m, c, 1.0 / D, None, ALU.mult))
                w8("vector", t_sq)
                vf(lambda e: e.tensor_add(stat[:, 9:10], sqc, stat[:, 8:9]))
                vf(lambda e, o=ex2c: e.tensor_scalar(o, stat[:, 9:10], 1.0 / D, None, ALU.mult))
                vf(lambda e, m=muc, o=mu2c: e.tensor_mul(o, m, m))
                t_var = op("vector", "sV", lambda e, a=ex2c, b=mu2c, o=varc: e.tensor_sub(o, a, b))
                w8("vector", t_var)
                w8("scalar", t_var)
                t_sd = op("scalar", "sA", lambda e, v=varc, o=stat[:, 7:8]:
                          e.activation(o, v, ACT.Sqrt, bias=1e-5))
                w8("vector", t_sd)
                vf(lambda e, o=rsc: e.reciprocal(o, stat[:, 7:8]))
                w8("vector", last_tr[0])
                op("vector", None, lambda e, x_=xt, o=xn_s[:, :], m=muc:
                   e.tensor_scalar(o, x_, m, None, ALU.subtract))
                t_n = op("vector", "sV", lambda e, o=xn_s[:, :], r=rsc:
                         e.tensor_scalar(o, o, r, None, ALU.mult))
                last_nrm[bi] = t_n
                do_transposes(t_n, bi, tc)
            return t_last[0]

        def emit_proj(psegs, wh_l, wl_l, xh_t, xl_t, nseg, want_sem):
            t_mm = None
            for seg in range(nseg):
                pp = psegs[seg]
                first = True
                for kc in range(16):
                    xh = xh_t[:, kc * TC + seg * 512: kc * TC + (seg + 1) * 512]
                    xl = xl_t[:, kc * TC + seg * 512: kc * TC + (seg + 1) * 512]
                    for pi in range(3):
                        a_ = wh_l[kc] if pi != 1 else wl_l[kc]
                        b_ = xh if pi != 2 else xl
                        last = (kc == 15 and pi == 2)
                        t_mm = op("tensor", want_sem if (last and seg == nseg - 1) else None,
                                  lambda e, p=pp, aa=a_, bb=b_, st=first, sp=last:
                                  e.matmul(p, aa, bb, start=st, stop=sp))
                        first = False
            return t_mm

        def emit_half(x_dram, full):
            t_xn = emit_ln_transposes(x_dram, full, permuted=False)

            t_wab = dma("sync", "sW", wbuf[0].ap()[:, 0:1024].rearrange("p (c f) -> p c f", f=64),
                        wab.rearrange("(c p) s -> p c s", p=128))
            w8("vector", t_wab)
            op("vector", None, lambda e: e.tensor_copy(wsp_hi[0][:, 0:1024], wbuf[0][:, 0:1024]))
            t_ws = op("vector", "sV", lambda e: e.scalar_tensor_tensor(
                wsp_lo[0][:, 0:1024], wbuf[0][:, 0:1024], 1.0, wsp_hi[0][:, 0:1024], ALU.mult, ALU.subtract))
            w8("tensor", t_ws)
            w8("tensor", t_xn)
            psegs = (PD[0][0:64, 0:512], PD[0][0:64, 512:1024])
            wha = [wsp_hi[0][:, kc * 64:(kc + 1) * 64] for kc in range(16)]
            wla = [wsp_lo[0][:, kc * 64:(kc + 1) * 64] for kc in range(16)]
            t_abmm = emit_proj(psegs, wha, wla, xnT_hi, xnT_lo, 2, "sP")
            w8("scalar", t_abmm)

            def af(f):
                w8("scalar", op("scalar", "sA", f))
            for seg in range(2):
                op("scalar", None, lambda e, s_=seg: e.activation(
                    ab[64:80, s_ * 512:(s_ + 1) * 512], PD[0][0:16, s_ * 512:(s_ + 1) * 512],
                    ACT.Identity, bias=c_bab[0:16, :]))
                af(lambda e, s_=seg: e.activation(
                    cdF[:, s_ * 512:(s_ + 1) * 512], PD[0][32:48, s_ * 512:(s_ + 1) * 512],
                    ACT.Identity, bias=c_bab[32:48, :]))
            af(lambda e: e.activation(ab[64:80, :], ab[64:80, :], ACT.Sigmoid))
            af(lambda e: e.activation(ab[0:16, :], ab[64:80, :], ACT.Ln, bias=1e-6))
            af(lambda e: e.activation(cdF[:, :], cdF[:, :], ACT.Exp))
            t_sp = op("scalar", "sA", lambda e: e.activation(ab[32:48, :], cdF[:, :], ACT.Ln, bias=1.0))
            w8("vector", t_sp)
            t_scan = None
            for c in range(NCHK):
                t_scan = op("vector", "sV", lambda e, c_=c: e.tensor_tensor_scan(
                    ab[0:16, c_ * 128:(c_ + 1) * 128], ab[0:16, c_ * 128:(c_ + 1) * 128],
                    c_zz[0:16, :], 0.0, ALU.add, ALU.add))
            w8("scalar", t_scan)
            t_cdF = op("scalar", "sA", lambda e: e.activation(cdF[:, :], ab[0:16, :], ACT.Exp))
            t_prep = None
            t_ce_prev = None
            for c in range(NCHK):
                w8("tensor", t_scan)
                w8("tensor", tk("sV"))
                t_tr0 = op("tensor", "sP", lambda e, c_=c: e.transpose(
                    PS[1][:, 0:16], ab[0:16, c_ * 128:(c_ + 1) * 128], c_id[0:16, 0:16]))
                t_tr = op("tensor", "sP", lambda e, c_=c: e.transpose(
                    PS[1][:, 16:32], ab[32:48, c_ * 128:(c_ + 1) * 128], c_id[32:48, 32:48]))
                w8("tensor", t_ce_prev)
                t_ce = op("tensor", "sP", lambda e, c_=c: e.transpose(
                    PS[2][0:1, 0:16], ab[0:16, c_ * 128 + 127:c_ * 128 + 128], c_id[0:16, 0:16]))
                w8("vector", t_tr)
                op("vector", None, lambda e, c_=c: e.tensor_copy(abT[:, c_ * 32:(c_ + 1) * 32], PS[1][:, 0:32]))
                w8("vector", t_ce)
                t_cer = op("vector", "sV", lambda e, c_=c: e.tensor_copy(
                    ce_row[:, c_ * 16:(c_ + 1) * 16], PS[2][0:1, 0:16]))
                t_ce_prev = t_cer
                w8("tensor", t_cer)
                t_bc = op("tensor", "sP", lambda e, c_=c: e.matmul(
                    PS[3][:, 0:16], c_or[:, :], ce_row[:, c_ * 16:(c_ + 1) * 16], start=True, stop=True))
                w8("vector", t_bc)
                w8("vector", t_cer)
                t_dp = op("vector", "sV", lambda e, c_=c: e.scalar_tensor_tensor(
                    dteT[:, c_ * 16:(c_ + 1) * 16], abT[:, c_ * 32:c_ * 32 + 16], -1.0,
                    PS[3][:, 0:16], ALU.mult, ALU.add))
                w8("scalar", t_dp)
                op("scalar", None, lambda e, c_=c: e.activation(
                    dteT[:, c_ * 16:(c_ + 1) * 16], dteT[:, c_ * 16:(c_ + 1) * 16], ACT.Exp))
                t_td = op("scalar", "sA", lambda e, c_=c: e.activation(
                    tdT[:, c_ * 16:(c_ + 1) * 16], PS[3][:, 0:16], ACT.Exp))
                w8("vector", t_td)
                t_prep = op("vector", "sV", lambda e, c_=c: e.tensor_mul(
                    dteT[:, c_ * 16:(c_ + 1) * 16], abT[:, c_ * 32 + 16:c_ * 32 + 32],
                    dteT[:, c_ * 16:(c_ + 1) * 16]))

            mats = ["k", "v"] + (["q"] if full else [])
            wsrc = {"k": wk, "v": wv, "q": wq}
            bsrc = {"k": bk, "v": bv, "q": bq}
            moff = {"k": 0, "v": 1, "q": 2}
            outsb = {"k": kT, "v": vT, "q": qT}
            nmat = len(mats)
            t_wld = {}
            t_wsp_used = [t_ws, t_ws]
            t_wbuf_used = [t_ws]

            def load_w(h, m):
                w8("sync", t_wbuf_used[0])
                t_wld[(h, m)] = dma("sync", "sWl",
                                    wbuf[0].ap().rearrange("p (c f) -> p c f", f=128),
                                    wsrc[m][:, h * 128:(h + 1) * 128].rearrange("(c p) s -> p c s", p=128))

            order = [(h, m) for h in range(NH) for m in mats]
            load_w(*order[0])
            t_pev = {}
            t_head_mm = {}
            for oi, (h, m) in enumerate(order):
                slot = oi % 2
                mo = moff[m]
                w8("vector", t_wld[(h, m)])
                w8("vector", t_wsp_used[slot])
                op("vector", None, lambda e, b=wbuf[0], s_=slot: e.tensor_copy(
                    wsp_hi[s_][:, :], b[:, :]))
                t_ws2 = op("vector", "sV", lambda e, b=wbuf[0], s_=slot: e.scalar_tensor_tensor(
                    wsp_lo[s_][:, :], b[:, :], 1.0, wsp_hi[s_][:, :], ALU.mult, ALU.subtract))
                t_wbuf_used[0] = t_ws2
                if oi + 1 < len(order):
                    load_w(*order[oi + 1])
                if m == "k":
                    w8("sync", tk("sA"))
                    for m2 in mats:
                        dma("sync", "sW", bcol[:, moff[m2]:moff[m2] + 1],
                            bsrc[m2][h * 128:(h + 1) * 128, :])
                t_b = tk("sW")
                w8("tensor", t_ws2)
                w8("tensor", tk("sA"))
                psegs = (PD[0][:, 0:512], PD[0][:, 512:1024])
                wha = [wsp_hi[slot][:, kc * 128:(kc + 1) * 128] for kc in range(16)]
                wla = [wsp_lo[slot][:, kc * 128:(kc + 1) * 128] for kc in range(16)]
                t_mm = emit_proj(psegs, wha, wla, xnT_hi, xnT_lo, 2, "sP")
                t_wsp_used[slot] = t_mm
                w8("scalar", t_mm)
                w8("scalar", t_b)
                for seg in range(2):
                    t_pev[m] = op("scalar", "sA", lambda e, s_=seg, o=outsb[m], mo_=mo:
                                  e.activation(o[:, s_ * 512:(s_ + 1) * 512],
                                               PD[0][:, s_ * 512:(s_ + 1) * 512],
                                               ACT.Identity, bias=bcol[:, mo_:mo_ + 1]))
                if m != mats[-1]:
                    continue
                if full:
                    w8("sync", t_head_mm.get(h - 1))
                    dma("sync", "sW", c_sel.ap(), sel16[:, h * 128:(h + 1) * 128])
                    t_selh = dma("sync", "sW", c_nsl.ap(), nsel16[:, h * 128:(h + 1) * 128])
                w8("scalar", t_pev["k"])
                t_ksq = op("scalar", "sA", lambda e: e.activation(ksq[:, :], kT[:, :], ACT.Square))
                t_rq = None
                for seg in range(2):
                    w8("tensor", t_ksq)
                    w8("tensor", t_rq)
                    t_rm = op("tensor", "sP", lambda e, s_=seg: e.matmul(
                        PS[2][0:1, 0:512], c_oc[:, :], ksq[:, s_ * 512:(s_ + 1) * 512],
                        start=True, stop=True))
                    w8("scalar", t_rm)
                    t_rq = op("scalar", "sA", lambda e, s_=seg: e.activation(
                        rn_row[:, s_ * 512:(s_ + 1) * 512], PS[2][0:1, 0:512], ACT.Sqrt))
                w8("vector", t_rq)
                t_mx = op("vector", "sV", lambda e: e.tensor_scalar(
                    rn_row[:, :], rn_row[:, :], 1e-12, None, ALU.max))
                w8("vector", t_mx)
                t_rn = op("vector", "sV", lambda e: e.reciprocal(rn_row[:, :], rn_row[:, :]))
                t_rc = None
                for c in range(NCHK):
                    w8("tensor", t_rn)
                    w8("tensor", t_rc)
                    t_rt = op("tensor", "sP", lambda e, c_=c: e.transpose(
                        PS[1][:, 0:1], rn_row[:, c_ * 128:(c_ + 1) * 128], c_id[0:1, 0:1]))
                    w8("vector", t_rt)
                    t_rc = op("vector", "sV", lambda e, c_=c: e.tensor_copy(
                        rn_cols[:, c_:c_ + 1], PS[1][:, 0:1]))
                t_state_up = None
                for c in range(NCHK):
                    cs0, cs1 = c * 128, (c + 1) * 128
                    w8("tensor", t_pev["v"])
                    w8("tensor", tk("sV"))
                    t_kt = op("tensor", "sP", lambda e, a=cs0, b=cs1: e.transpose(
                        PD[0][:, 0:128], kT[:, a:b], c_id[:, :]))
                    t_vt = op("tensor", "sP", lambda e, a=cs0, b=cs1: e.transpose(
                        PD[0][:, 512:640], vT[:, a:b], c_id[:, :]))
                    w8("vector", t_kt)
                    w8("vector", t_rc)
                    op("vector", None, lambda e, c_=c: e.tensor_scalar(
                        k_tok[:, c_ * 128:(c_ + 1) * 128], PD[0][:, 0:128],
                        rn_cols[:, c_:c_ + 1], None, ALU.mult))
                    w8("vector", t_vt)
                    t_vc = op("vector", "sV", lambda e, c_=c: e.tensor_copy(
                        v_tok[:, c_ * 128:(c_ + 1) * 128], PD[0][:, 512:640]))
                    w8("vector", t_prep)
                    t_brn = op("vector", "sV", lambda e, c_=c, h_=h: e.tensor_mul(
                        brn_c[:, 0:1], abT[:, c_ * 32 + 16 + h_:c_ * 32 + 17 + h_],
                        rn_cols[:, c_:c_ + 1]))
                    w8("vector", t_brn)
                    t_kw = op("vector", "sV", lambda e, c_=c, h_=h: e.tensor_scalar(
                        kwt[:, :], k_tok[:, c_ * 128:(c_ + 1) * 128],
                        sdT[:, c_ * 16 + h_:c_ * 16 + h_ + 1], None, ALU.mult))
                    if full:
                        w8("tensor", t_pev["q"])
                        w8("tensor", t_selh)
                        w8("tensor", t_vc)
                        t_sc_mm = op("tensor", "sP", lambda e, a=cs0, b=cs1: e.matmul(
                            PS[0][:, 0:128], kT[:, a:b], qT[:, a:b], start=True, stop=True))
                        op("tensor", None, lambda e, c_=c, h_=h: e.matmul(
                            PS[1][:, 0:128], c_sel[:, :],
                            ab[0:16, c_ * 128:(c_ + 1) * 128], start=True, stop=False))
                        t_db = op("tensor", "sP", lambda e, c_=c, h_=h: e.matmul(
                            PS[1][:, 0:128], ab[0:16, c_ * 128:(c_ + 1) * 128],
                            c_nsl[:, :], start=False, stop=True))
                        w8("tensor", t_cdF)
                        t_cb = op("tensor", "sP", lambda e, c_=c, h_=h: e.matmul(
                            PS[2][:, 0:128], c_sel[:, :],
                            cdF[0:16, c_ * 128:(c_ + 1) * 128], start=True, stop=True))
                        w8("vector", t_db)
                        t_dm = op("vector", "sV", lambda e: e.scalar_tensor_tensor(
                            dmt[:, :], PS[1][:, 0:128], 1.0, c_mask[:, :], ALU.mult, ALU.add))
                        w8("scalar", t_dm)
                        t_dk = op("scalar", "sA", lambda e: e.activation(dket[:, :], dmt[:, :], ACT.Exp))
                        w8("vector", t_sc_mm)
                        w8("vector", t_dk)
                        t_at = op("vector", "sV", lambda e: e.scalar_tensor_tensor(
                            attnT[:, :], PS[0][:, 0:128], brn_c[:, 0:1], dket[:, :],
                            ALU.mult, ALU.mult))
                        w8("vector", t_cb)
                        t_qc = op("vector", "sV", lambda e, a=cs0, b=cs1: e.tensor_tensor(
                            q_cd[:, :], qT[:, a:b], PS[2][:, 0:128], ALU.mult))
                        w8("tensor", t_at)
                        w8("tensor", t_qc)
                        w8("tensor", t_state_up)
                        op("tensor", None, lambda e, a=cs0, b=cs1: e.matmul(
                            PS[3][:, 0:128], v_tok[:, a:b], attnT[:, :], start=True, stop=False))
                        t_ym = op("tensor", "sP", lambda e, h_=h: e.matmul(
                            PS[3][:, 0:128], state[:, h_ * 128:(h_ + 1) * 128], q_cd[:, :],
                            start=False, stop=True))
                        w8("vector", t_ym)
                        t_ye = op("vector", "sV", lambda e: e.tensor_copy(y_sb[:, :], PS[3][:, 0:128]))
                        if dbg_ch is not None and dbg_ch == (c, h):
                            srcsel = {"y": y_sb[:, :], "q_cd": q_cd[:, :], "attnT": attnT[:, :],
                                      "state": state[:, h * 128:(h + 1) * 128],
                                      "v": v_tok[:, cs0:cs1], "kw": kwt[:, :],
                                      "scores": PS[0][:, 0:128], "brn": brn_c[:, 0:1],
                                      "dk": dket[:, :]}[dbg_what] if dbg_what != "rncols" else None
                            if dbg_what == "rncols":
                                op("vector", "sV", lambda e: e.tensor_copy(
                                    dbg_grab[:, 0:NCHK], rn_cols[:, :]))
                            else:
                                op("vector", "sV", lambda e, s_=srcsel: e.tensor_copy(dbg_grab[:, :], s_))
                        w8("tensor", t_ye)
                        t_yt = op("tensor", "sP", lambda e: e.transpose(
                            PD[1][:, 0:128], y_sb[:, :], c_id[:, :]))
                        w8("vector", t_yt)
                        op("vector", "sV", lambda e, c_=c, h_=h: e.tensor_tensor(
                            A_x[:, c_ * D + h_ * 128: c_ * D + (h_ + 1) * 128],
                            A_x[:, c_ * D + h_ * 128: c_ * D + (h_ + 1) * 128],
                            PD[1][:, 0:128], ALU.add))
                    w8("tensor", t_kw)
                    w8("tensor", t_state_up)
                    t_dl = op("tensor", "sP", lambda e, a=cs0, b=cs1: e.matmul(
                        PD[1][:, 512:640], kwt[:, :], v_tok[:, a:b], start=True, stop=True))
                    w8("vector", t_dl)
                    t_state_up = op("vector", "sV", lambda e, c_=c, h_=h: e.scalar_tensor_tensor(
                        state[:, h_ * 128:(h_ + 1) * 128], state[:, h_ * 128:(h_ + 1) * 128],
                        tdT[:, c_ * 16 + h_:c_ * 16 + h_ + 1], PD[1][:, 512:640],
                        ALU.mult, ALU.add))
                t_head_mm[h] = tk("sP")
                if not full:
                    w8("vector", t_const)
                    w8("vector", t_state_up)
                    op("vector", "sV", lambda e, h_=h: e.tensor_scalar(
                        state[:, h_ * 128:(h_ + 1) * 128], state[:, h_ * 128:(h_ + 1) * 128],
                        c_flag[:, 0:1], None, ALU.mult))

        emit_half(xF, False)
        if stop_after >= 1:
            emit_half(xO, True)

        if stop_after >= 2:
            emit_ln_transposes(None, True, permuted=True)

        for (tname, tshape, tdt) in taps:
            if tname in ("kT", "vT", "qT", "k_tok", "v_tok", "cla", "sp", "sig", "dbg_grab", "state", "cdF",
                         "dteT", "tdT", "sdT", "abT", "attnT", "rn_row", "A_x",
                         "xnT_hi", "xnT_lo"):
                srcs = dict(kT=kT, vT=vT, qT=qT, k_tok=k_tok, v_tok=v_tok,
                            dbg_grab=c_mask,
                            cla=ab[0:16, :], sp=ab[32:48, :], sig=ab[64:80, :],
                            state=state, cdF=cdF, dteT=dteT, tdT=tdT, sdT=sdT, abT=abT,
                            attnT=attnT, rn_row=rn_row)
                if tname == "A_x":
                    src_ap = A_x[:, 0:tshape[1]]
                elif tname in ("xnT_hi", "xnT_lo"):
                    src_ap = (xnT_hi if tname == "xnT_hi" else xnT_lo)[:, 0:tshape[1]]
                else:
                    src_ap = srcs[tname][0:tshape[0], 0:tshape[1]]
                w8("sync", tk("sV"))
                w8("sync", tk("sP"))
                w8("sync", tk("sA"))
                dma("sync", "sO", tap_dram[tname], src_ap)
                tap_sbuf[tname] = True

        astk.close()

        t_flush_last = {}
        if stop_after >= 3:
            fstk = ctx.enter_context(contextlib.ExitStack())
            it_hi = sb("it_hi", [128, 4 * TC], FP16, fstk)
            it_lo = sb("it_lo", [128, 4 * TC], FP16, fstk)
            it32 = sb("it32", [128, TC], F32, fstk)
            w1g = sb("w1g", [128, 2048], FP16, fstk)
            w2g = sb("w2g", [128, 2048], FP16, fstk)
            w1u = sb("w1u", [128, 2048], FP16, fstk)
            w2u = sb("w2u", [128, 2048], FP16, fstk)
            w1d = [[sb(f"w1d{i}_{p}", [128, 512], FP16, fstk) for p in range(2)] for i in range(4)]
            w2d = [[sb(f"w2d{i}_{p}", [128, 512], FP16, fstk) for p in range(2)] for i in range(4)]
            pkb = [sb(f"pkb{i}", [128, 1024], I32, fstk) for i in range(2)]
            srb = [sb(f"srb{i}", [128, 1024], F32, fstk) for i in range(2)]
            pkd = sb("pkd", [128, 512], I32, fstk)
            srd = sb("srd", [128, 512], F32, fstk)
            DQW = 512
            shi = sb("shi", [128, DQW], BF16, fstk)
            slo = sb("slo", [128, DQW], BF16, fstk)
            dq_c = sb("dq_c", [128, DQW], I32, fstk)
            dq_s = sb("dq_s", [128, DQW], I32, fstk)
            dq_t1 = sb("dq_t1", [128, DQW], BF16, fstk)

            t_attn_done_v = tk("sV")
            t_attn_done_p = tk("sP")
            t_attn_done_a = tk("sA")
            for eng in ("gpsimd",):
                w8(eng, t_attn_done_v)
                w8(eng, t_attn_done_p)
                w8(eng, t_attn_done_a)

            def emit_dq_plane(pk_ap, sr_ap, plane, w1_ap, w2_ap, do_scale, t_sr=None,
                              t_shi_free=None):
                W = DQW
                if plane == 0:
                    op("vector", None, lambda e: e.tensor_scalar(dq_c[:, 0:W], pk_ap, 7, None,
                                                                 ALU.bitwise_and))
                    op("vector", None, lambda e: e.tensor_scalar(dq_s[:, 0:W], pk_ap, 8, 2,
                                                                 ALU.bitwise_and, ALU.logical_shift_right))
                else:
                    op("vector", None, lambda e: e.tensor_scalar(dq_c[:, 0:W], pk_ap, 4, 7,
                                                                 ALU.logical_shift_right, ALU.bitwise_and))
                    op("vector", None, lambda e: e.tensor_scalar(dq_s[:, 0:W], pk_ap, 128, 6,
                                                                 ALU.bitwise_and, ALU.logical_shift_right))
                op("vector", None, lambda e: e.tensor_scalar(
                    dq_t1[:, 0:W], dq_c[:, 0:W], -4.0, 2.0, ALU.add, ALU.mult))
                op("vector", None, lambda e: e.scalar_tensor_tensor(
                    dq_t1[:, 0:W], dq_c[:, 0:W], -2.0, dq_t1[:, 0:W], ALU.add, ALU.max))
                op("vector", None, lambda e: e.scalar_tensor_tensor(
                    dq_t1[:, 0:W], dq_c[:, 0:W], 0.5, dq_t1[:, 0:W], ALU.mult, ALU.max))
                op("vector", None, lambda e: e.scalar_tensor_tensor(
                    dq_t1[:, 0:W], dq_s[:, 0:W], -1.0, dq_t1[:, 0:W], ALU.add, ALU.mult))
                if do_scale:
                    w8("scalar", t_sr)
                    w8("scalar", t_shi_free)
                    t_shi = op("scalar", "sA", lambda e: e.activation(
                        shi[:, 0:W], sr_ap, ACT.Copy, scale=-1.0))
                    w8("vector", t_sr)
                    w8("vector", t_shi)
                    op("vector", None, lambda e: e.scalar_tensor_tensor(slo[:, 0:W], sr_ap, -1.0,
                                                                        shi[:, 0:W], ALU.mult, ALU.subtract))
                op("vector", None, lambda e: e.tensor_tensor(w1_ap, dq_t1[:, 0:W], shi[:, 0:W], ALU.mult))
                return op("vector", "sV", lambda e: e.tensor_tensor(w2_ap, dq_t1[:, 0:W], slo[:, 0:W], ALU.mult))

            t_pk_prev = [None, None]
            t_dq_last = [None]

            def emit_gu_nchunk(mat_pk, mat_sc, nn, bufi, w1b, w2b, t_wfree):
                n0 = nn * 128
                w8("gpsimd", t_pk_prev[bufi])
                strm = "sGt" if bufi == 0 else "sU"
                t_pk = dma("gpsimd", strm, pkb[bufi].ap().rearrange("p (c f) -> p c f", f=128),
                           mat_pk[:, n0:n0 + 128].rearrange("(c p) s -> p c s", p=128))
                t_sr = dma("gpsimd", strm, srb[bufi].ap().rearrange("p (c f) -> p c f", f=128),
                           mat_sc[:, n0:n0 + 128].rearrange("(c p) s -> p c s", p=128))
                w8("vector", t_pk)
                w8("vector", t_sr)
                w8("vector", t_wfree)
                t_dq = t_dq_last[0]
                for s0 in range(0, 1024, DQW):
                    for plane in (0, 1):
                        t_dq = emit_dq_plane(
                            pkb[bufi].ap()[:, s0:s0 + DQW], srb[bufi].ap()[:, s0:s0 + DQW], plane,
                            w1b.ap()[:, plane * 1024 + s0: plane * 1024 + s0 + DQW],
                            w2b.ap()[:, plane * 1024 + s0: plane * 1024 + s0 + DQW],
                            plane == 0, t_sr=t_sr, t_shi_free=t_dq)
                t_dq_last[0] = t_dq
                t_pk_prev[bufi] = t_dq
                return t_dq

            t_pkd_prev = [None]
            t_dn_used = {}
            t_dq_tile = {}

            slot_pk = [pkd.ap()[:, 0:512], pkb[1].ap()[:, 0:512],
                       pkb[1].ap()[:, 512:1024], pkb[0].ap()[:, 0:512]]
            slot_sr = [srd.ap()[:, 0:512], srb[1].ap()[:, 0:512],
                       srb[1].ap()[:, 512:1024], srb[0].ap()[:, 0:512]]
            slot_sem = ["sD0", "sD1", "sD2", "sD3"]
            t_slot_dq = [None, None, None, None]

            def emit_down_tile(mc, hq, mci, par):
                plane = 0 if mc < 64 else 1
                jc = mc if plane == 0 else mc - 64
                h0 = hq * 512
                s = mci
                w8("gpsimd", t_slot_dq[s])
                if s in (1, 2):
                    w8("gpsimd", t_pk_prev[1])
                if s == 3:
                    w8("gpsimd", t_pk_prev[0])
                t_pk = dma("gpsimd", slot_sem[s], slot_pk[s],
                           dpk[jc * 128:(jc + 1) * 128, h0:h0 + 512])
                t_sr = dma("gpsimd", slot_sem[s], slot_sr[s],
                           dsc[jc * 128:(jc + 1) * 128, h0:h0 + 512])
                w8("vector", t_sr)
                w8("vector", t_dn_used.get((mci, par)))
                t_dq = emit_dq_plane(slot_pk[s], slot_sr[s], plane,
                                     w1d[mci][par].ap()[:, 0:512],
                                     w2d[mci][par].ap()[:, 0:512], True,
                                     t_sr=t_sr, t_shi_free=t_dq_last[0])
                t_dq_last[0] = t_dq
                t_slot_dq[s] = t_dq
                if s in (1, 2):
                    t_pk_prev[1] = t_dq
                if s == 3:
                    t_pk_prev[0] = t_dq
                t_dq_tile[(mci, par)] = t_dq

            NGRP = 32
            t_gu_used = {"g": None, "u": None}
            t_dq_g = emit_gu_nchunk(gpk, gsc, 0, 0, w1g, w2g, None)
            t_dq_u = None
            t_pd_reader = [t_attn_done_v, t_attn_done_v]
            t_it32_add = [None, None]
            for g in range(NGRP):
                for nli in range(4):
                    nn = g * 4 + nli
                    t_dq_u = emit_gu_nchunk(upk, usc, nn, 1, w1u, w2u, t_gu_used["u"])
                    pg = PD[nli % 2]
                    pu = PD[1 - nli % 2]
                    w8("tensor", t_dq_g)
                    w8("tensor", t_pd_reader[nli % 2])
                    psegs = (pg[:, 0:512], pg[:, 512:1024])
                    wha = [w1g[:, kc * 128:(kc + 1) * 128] for kc in range(16)]
                    wla = [w2g[:, kc * 128:(kc + 1) * 128] for kc in range(16)]
                    t_mm_g = emit_proj(psegs, wha, wla, xnT_hi, xnT_lo, 2, "sP")
                    t_gu_used["g"] = t_mm_g
                    w8("scalar", t_mm_g)
                    w8("scalar", t_it32_add[0])
                    w8("scalar", t_it32_add[1])
                    t_gl = op("scalar", "sA", lambda e, p=pg: e.activation(
                        it32[:, :], p[:, 0:1024], ACT.Gelu_apprx_tanh))
                    t_pd_reader[nli % 2] = t_gl
                    if nn + 1 < MLP // 128:
                        t_dq_g = emit_gu_nchunk(gpk, gsc, nn + 1, 0, w1g, w2g, t_gu_used["g"])
                    w8("tensor", t_dq_u)
                    w8("tensor", t_pd_reader[1 - nli % 2])
                    psegs = (pu[:, 0:512], pu[:, 512:1024])
                    wha = [w1u[:, kc * 128:(kc + 1) * 128] for kc in range(16)]
                    wla = [w2u[:, kc * 128:(kc + 1) * 128] for kc in range(16)]
                    t_mm_u = emit_proj(psegs, wha, wla, xnT_hi, xnT_lo, 2, "sP")
                    t_gu_used["u"] = t_mm_u
                    w8("vector", t_gl)
                    w8("vector", t_mm_u)
                    t_i32 = op("vector", "sV", lambda e, p=pu: e.tensor_tensor(
                        it32[:, :], it32[:, :], p[:, 0:1024], ALU.mult))
                    t_pd_reader[1 - nli % 2] = t_i32
                    w8("scalar", t_i32)
                    t_ih = op("scalar", "sA", lambda e, n_=nli: e.activation(
                        it_hi[:, n_ * TC:(n_ + 1) * TC], it32[:, :], ACT.Copy))
                    w8("vector", t_ih)
                    op("vector", "sV", lambda e, n_=nli: e.scalar_tensor_tensor(
                        it_lo[:, n_ * TC:(n_ + 1) * TC], it32[:, :], 1.0,
                        it_hi[:, n_ * TC:(n_ + 1) * TC], ALU.mult, ALU.subtract))
                t_inter = tk("sV")
                for hq in range(4):
                    d = g * 4 + hq
                    par = d % 2
                    if d == 0:
                        for mci in range(4):
                            emit_down_tile(mci, 0, mci, 0)
                    nxt = (g, hq + 1) if hq < 3 else ((g + 1, 0) if g + 1 < NGRP else None)
                    for tc in range(NCHK):
                        pdd = PD[tc % 2]
                        w8("tensor", t_pd_reader[tc % 2])
                        w8("tensor", t_inter)
                        t_mm_d = None
                        for mci in range(4):
                            w8("tensor", t_dq_tile[(mci, par)])
                            for pi in range(3):
                                wdd = (w1d[mci][par] if pi != 1 else w2d[mci][par]).ap()[:, 0:512]
                                itt = (it_hi if pi != 2 else it_lo).ap()[
                                    :, mci * TC + tc * 128: mci * TC + (tc + 1) * 128]
                                last = (mci == 3 and pi == 2)
                                t_mm_d = op("tensor", "sP" if last else None,
                                            lambda e, p=pdd[:, 0:512], w_=wdd,
                                            i_=itt, st=(mci == 0 and pi == 0), sp=last:
                                            e.matmul(p, i_, w_, start=st, stop=sp))
                        w8("scalar", t_mm_d)
                        w8("scalar", t_it32_add[tc % 2])
                        t_ev = op("scalar", "sA", lambda e, t_=tc, p=pdd: e.activation(
                            it32[:, (t_ % 2) * 512:(t_ % 2) * 512 + 512], p[:, 0:512], ACT.Copy))
                        w8("gpsimd", t_ev)
                        t_fl = op("gpsimd", "sGc", lambda e, t_=tc, h_=hq: e.tensor_tensor(
                            A_x[:, t_ * D + h_ * 512: t_ * D + (h_ + 1) * 512],
                            A_x[:, t_ * D + h_ * 512: t_ * D + (h_ + 1) * 512],
                            it32[:, (t_ % 2) * 512:(t_ % 2) * 512 + 512], ALU.add))
                        t_it32_add[tc % 2] = t_fl
                        t_pd_reader[tc % 2] = t_ev
                        t_flush_last[(hq % 2, tc)] = t_fl
                        if nxt is not None and 2 <= tc <= 5:
                            ng, nhq = nxt
                            emit_down_tile(ng * 4 + (tc - 2), nhq, tc - 2, 1 - par)
                    t_dn_new = tk("sP")
                    for mci in range(4):
                        t_dn_used[(mci, par)] = t_dn_new
            for (tname, tshape, tdt) in taps:
                if tname in tap_sbuf:
                    continue
                srcs = dict(it_hi=it_hi, it_lo=it_lo, w1g=w1g, w2g=w2g, w1d0=w1d[0][0])
                if tname not in srcs:
                    continue
                w8("sync", tk("sV"))
                w8("sync", tk("sP"))
                dma("sync", "sO", tap_dram[tname], srcs[tname][0:tshape[0], 0:tshape[1]])
                tap_sbuf[tname] = True

        for tc in range(NCHK):
            if stop_after >= 3:
                w8("sync", t_flush_last[(0, tc)])
                w8("sync", t_flush_last[(1, tc)])
            else:
                w8("sync", tk("sV"))
                w8("sync", tk("sP"))
                w8("sync", tk("sA"))
            dma("sync", "sO", out[tc * 128:(tc + 1) * 128, :], A_x[:, tc * D:(tc + 1) * D])

        em("gpsimd", lambda e: e.wait_ge(SEMH["sO"], cnt["sO"]))

        block = ctx.enter_context(nc.Block())

        def runner(name):
            def run(e):
                for fn in Q[name]:
                    fn(e)
            return run

        block.sync(runner("sync"))
        block.tensor(runner("tensor"))
        block.vector(runner("vector"))
        block.scalar(runner("scalar"))
        block.gpsimd(runner("gpsimd"))

    return nc


_SEL16 = np.ascontiguousarray(
    (np.arange(16)[:, None, None] == np.arange(16)[None, :, None])
    .astype(np.float32).repeat(128, axis=2).reshape(16, 16 * 128))


def _host_prep(inputs):
    x = np.asarray(inputs["L0_attn_x"], dtype=np.float32)
    perm = np.concatenate([np.arange(0, MLP, 2), np.arange(1, MLP, 2)])
    rep16 = np.repeat(np.arange(64), 16)
    rep16d = np.repeat(np.arange(512), 16)
    gpkT = np.ascontiguousarray(np.asarray(inputs["L0_ffn_gate_W_packed"], np.int32).T[:, perm])
    upkT = np.ascontiguousarray(np.asarray(inputs["L0_ffn_up_W_packed"], np.int32).T[:, perm])
    dpkT = np.ascontiguousarray(np.asarray(inputs["L0_ffn_down_W_packed"], np.int32).T)
    gscT = np.ascontiguousarray(np.asarray(inputs["L0_ffn_gate_scales"], np.float32).T[rep16][:, perm])
    uscT = np.ascontiguousarray(np.asarray(inputs["L0_ffn_up_scales"], np.float32).T[rep16][:, perm])
    dscT = np.ascontiguousarray(np.asarray(inputs["L0_ffn_down_scales"], np.float32).T[rep16d])
    _z16 = np.zeros((D, 16), np.float32)
    wabf = np.ascontiguousarray(np.concatenate(
        [np.asarray(inputs["L0_attn_Walpha"], np.float32), _z16,
         np.asarray(inputs["L0_attn_Wbeta"], np.float32), _z16], axis=1))
    babf = np.ascontiguousarray(np.concatenate(
        [np.asarray(inputs["L0_attn_balpha"], np.float32), np.zeros(16, np.float32),
         np.asarray(inputs["L0_attn_bbeta"], np.float32), np.zeros(16, np.float32)])).reshape(64, 1)
    e = np.arange(128)
    mask = np.where(e[None, :] >= e[:, None], 0.0, NEG).astype(np.float32)
    shared = dict(
        wq=np.ascontiguousarray(np.asarray(inputs["L0_attn_Wq"], np.float32)),
        wk=np.ascontiguousarray(np.asarray(inputs["L0_attn_Wk"], np.float32)),
        wv=np.ascontiguousarray(np.asarray(inputs["L0_attn_Wv"], np.float32)),
        bq=np.asarray(inputs["L0_attn_bq"], np.float32).reshape(D, 1).copy(),
        bk=np.asarray(inputs["L0_attn_bk"], np.float32).reshape(D, 1).copy(),
        bv=np.asarray(inputs["L0_attn_bv"], np.float32).reshape(D, 1).copy(),
        wab=wabf, bab=babf,
        gpk=gpkT, upk=upkT, dpk=dpkT, gsc=gscT, usc=uscT, dsc=dscT,
        ident=np.eye(128, dtype=np.float32),
        ones_row=np.ones((1, 128), np.float32),
        ones_col=np.ones((128, 1), np.float32),
        maskneg=mask,
        sel16=_SEL16, nsel16=np.ascontiguousarray(-_SEL16),
        zz128=np.zeros((128, 128), np.float32),
        constv=np.concatenate([np.full((128, 1), 1e-5, np.float32),
                               np.full((128, 1), 1e-6, np.float32),
                               np.full((128, 1), 1.0, np.float32)], axis=1),
    )
    in_maps = []
    for c in range(8):
        b, own = c // 2, c % 2
        m = dict(shared)
        m["xO"] = np.ascontiguousarray(x[b, own * TC:(own + 1) * TC, :])
        m["xF"] = np.ascontiguousarray(x[b, 0:TC, :])
        m["flagF"] = np.full((128, 1), 1.0 if own == 1 else 0.0, np.float32)
        in_maps.append(m)
    return in_maps


def kernel(**inputs):
    in_maps = _host_prep(inputs)
    nc = build_program()
    res = run_bass_kernel_spmd(nc, in_maps, core_ids=list(range(8)))
    outp = np.empty((4, 2048, D), dtype=np.float32)
    for c in range(8):
        b, own = c // 2, c % 2
        outp[b, own * TC:(own + 1) * TC, :] = np.asarray(res.results[c]["out"])
    return outp


# version 11 (final)
# speedup vs baseline: 1.0996x; 1.0996x over previous
import contextlib
import numpy as np

import concourse.bass as bass
import concourse.mybir as mybir
from concourse.bass_utils import run_bass_kernel_spmd

F32 = mybir.dt.float32
BF16 = mybir.dt.bfloat16
FP16 = mybir.dt.float16
I32 = mybir.dt.int32
ALU = mybir.AluOpType
ACT = mybir.ActivationFunctionType
AXX = mybir.AxisListType.X

D = 2048
TC = 1024
NH = 16
NCHK = 8
MLP = 16384
NEG = -1.0e30
ENGS = ("sync", "tensor", "vector", "scalar", "gpsimd")


def build_program(stop_after=9, taps=(), dbg_ch=None, dbg_what="y"):
    nc = bass.Bass("TRN2", target_bir_lowering=False, detect_race_conditions=False)

    def din(name, shape, dt):
        return nc.dram_tensor(name, shape, dt, kind="ExternalInput").ap()

    xO = din("xO", [TC, D], F32)
    xF = din("xF", [TC, D], F32)
    flagF = din("flagF", [128, 1], F32)
    wq = din("wq", [D, D], F32)
    wk = din("wk", [D, D], F32)
    wv = din("wv", [D, D], F32)
    bq = din("bq", [D, 1], F32)
    bk = din("bk", [D, 1], F32)
    bv = din("bv", [D, 1], F32)
    wab = din("wab", [D, 64], F32)
    bab = din("bab", [64, 1], F32)
    gpk = din("gpk", [1024, MLP], I32)
    upk = din("upk", [1024, MLP], I32)
    dpk = din("dpk", [8192, D], I32)
    gsc = din("gsc", [1024, MLP], F32)
    usc = din("usc", [1024, MLP], F32)
    dsc = din("dsc", [8192, D], F32)
    ident = din("ident", [128, 128], F32)
    ones_row = din("ones_row", [1, 128], F32)
    ones_col = din("ones_col", [128, 1], F32)
    maskneg = din("maskneg", [128, 128], F32)
    sel16 = din("sel16", [16, NH * 128], F32)
    nsel16 = din("nsel16", [16, NH * 128], F32)
    zz128 = din("zz128", [128, 128], F32)
    constv = din("constv", [128, 3], F32)

    out = nc.dram_tensor("out", [TC, D], F32, kind="ExternalOutput").ap()
    tap_dram = {}
    for (tname, tshape, tdt) in taps:
        tap_dram[tname] = nc.dram_tensor("tap_" + tname, tshape, tdt, kind="ExternalOutput").ap()
    tap_sbuf = {}

    Q = {e: [] for e in ENGS}
    cnt = dict(sW=0, sG=0, sO=0, sP=0, sV=0, sA=0, sWm=0, sGm=0,
               sGt=0, sU=0, sD0=0, sD1=0, sD2=0, sD3=0, sWl=0, sGc=0)
    dma_sem = {"sync": "sW", "gpsimd": "sG"}
    ms_sem = {"sync": "sWm", "gpsimd": "sGm"}
    ms_covered = {"sync": 0, "gpsimd": 0}
    waited = {}

    def em(eng, fn):
        Q[eng].append(fn)

    def w8(eng, tok_):
        if tok_ is None:
            return
        if tok_[0] == "dma":
            _, queue, val = tok_
            if val > ms_covered[queue]:
                cur = cnt[dma_sem[queue]]
                em(queue, lambda e, s=dma_sem[queue], v=cur: e.wait_ge(SEMH[s], v))
                cnt[ms_sem[queue]] += 1
                em(queue, lambda e, s=ms_sem[queue]: e.sem_inc(SEMH[s], 1))
                ms_covered[queue] = cur
            tok_ = (ms_sem[queue], cnt[ms_sem[queue]])
        sem, val = tok_
        if val <= 0:
            return
        if waited.get((eng, sem), 0) >= val:
            return
        waited[(eng, sem)] = val
        em(eng, lambda e, s=sem, v=val: e.wait_ge(SEMH[s], v))

    def dma(eng, sem, out_ap, in_ap):
        sem = dma_sem.get(eng, sem) if sem in ("sW", "sG") else sem
        cnt[sem] += 16
        em(eng, lambda e, o=out_ap, i=in_ap, s=sem: e.dma_start(out=o, in_=i).then_inc(SEMH[s], 16))
        if sem in ("sO", "sGt", "sU", "sD0", "sD1", "sD2", "sD3", "sWl"):
            return (sem, cnt[sem])
        return ("dma", eng, cnt[sem])

    def op(eng, sem, fn):
        if sem is None:
            em(eng, fn)
            return None
        cnt[sem] += 1
        em(eng, lambda e, f=fn, s=sem: f(e).then_inc(SEMH[s], 1))
        return (sem, cnt[sem])

    def tk(sem):
        if sem == "sW":
            return ("dma", "sync", cnt["sW"])
        if sem == "sG":
            return ("dma", "gpsimd", cnt["sG"])
        return (sem, cnt[sem])

    with contextlib.ExitStack() as ctx:
        def sb(name, shape, dt, stack=None):
            return (stack or ctx).enter_context(nc.sbuf_tensor(name, shape, dt))

        SEMH = {s: ctx.enter_context(nc.semaphore(s)) for s in cnt}

        c_cv = sb("c_cv", [128, 3], F32)
        nc.const_aps.register(F32, 1e-5, c_cv[:, 0:1])
        nc.const_aps.register(F32, 1e-6, c_cv[:, 1:2])
        nc.const_aps.register(F32, 1.0, c_cv[:, 2:3])
        A_x = sb("A_x", [128, 8 * D], F32)
        xnT_hi = sb("xnT_hi", [128, 16 * TC], FP16)
        xnT_lo = sb("xnT_lo", [128, 16 * TC], FP16)

        PS = [ctx.enter_context(nc.psum_tensor(f"PS{i}", [128, 512], F32)) for i in range(4)]
        PD = [ctx.enter_context(nc.psum_tensor(f"PD{i}", [128, 1024], F32)) for i in range(2)]

        astk = ctx.enter_context(contextlib.ExitStack())
        c_id = sb("c_id", [128, 128], F32, astk)
        c_or = sb("c_or", [1, 128], F32, astk)
        c_oc = sb("c_oc", [128, 1], F32, astk)
        c_mask = sb("c_mask", [128, 128], F32, astk)
        c_zz = sb("c_zz", [16, 128], F32, astk)
        c_flag = sb("c_flag", [128, 1], F32, astk)
        c_bab = sb("c_bab", [64, 1], F32, astk)
        xn_s = sb("xn_s", [128, D], F32, astk)
        stat = sb("stat", [128, 16], F32, astk)
        wbuf0 = sb("wb0", [128, D], F32, astk)
        wbuf = [wbuf0, wbuf0]
        wsp_hi = [sb(f"wsph{i}", [128, D], FP16, astk) for i in range(2)]
        wsp_lo = [sb(f"wspl{i}", [128, D], FP16, astk) for i in range(2)]
        bcol = sb("bcol", [128, 4], F32, astk)
        qT = sb("qT", [128, TC], F32, astk)
        kT = sb("kT", [128, TC], F32, astk)
        vT = sb("vT", [128, TC], F32, astk)
        k_tok = sb("k_tok", [128, TC], F32, astk)
        v_tok = sb("v_tok", [128, TC], F32, astk)
        ksq = v_tok
        rn_row = sb("rn_row", [1, TC], F32, astk)
        rn_cols = sb("rn_cols", [128, NCHK], F32, astk)
        ab = sb("ab", [96, TC], F32, astk)
        cdF = sb("cdF", [16, TC], F32, astk)
        abT = sb("abT", [128, NCHK * 32], F32, astk)
        dteT = sb("dteT", [128, NCHK * 16], F32, astk)
        tdT = sb("tdT", [128, NCHK * 16], F32, astk)
        sdT = dteT
        kwt = sb("kwt", [128, 128], F32, astk)
        ce_row = rn_row
        dmt = sb("dmt", [128, 128], F32, astk)
        dket = dmt
        attnT = sb("attnT", [128, 128], F32, astk)
        q_cd = sb("q_cd", [128, 128], F32, astk)
        y_sb = sb("y_sb", [128, 128], F32, astk)
        state = sb("state", [128, NH * 128], F32, astk)
        brn_c = sb("brn_c", [128, 1], F32, astk)
        dbg_grab = c_mask if dbg_ch is not None else None
        c_sel = sb("c_sel", [16, 128], F32, astk)
        c_nsl = sb("c_nsl", [16, 128], F32, astk)

        t_const = None
        if stop_after < 1:
            t_dbgms = op("vector", "sV", lambda e: e.memset(A_x[:, :], 0.0))
            w8("sync", t_dbgms)
        for dst, src in ((c_id, ident), (c_or, ones_row), (c_oc, ones_col),
                         (c_mask, maskneg), (c_zz, zz128[0:16, :]), (c_flag, flagF), (c_bab, bab),
                         (c_cv, constv)):
            t_const = dma("sync", "sW", dst.ap(), src)
        for h in range(NH):
            t_const = dma("sync", "sW", state[:, h * 128:(h + 1) * 128], zz128)

        def emit_ln_transposes(x_dram, use_Ax, permuted):
            last_tr = [None]
            last_nrm = [None, None]
            t_last = [None]
            hi3 = xnT_hi.ap().rearrange("p (c f) -> p c f", f=TC)
            lo3 = xnT_lo.ap().rearrange("p (c f) -> p c f", f=TC)

            def do_transposes(t_n, bi, tc):
                if not permuted:
                    groups = [(bq_ * 4, [xn_s.ap()[:, (bq_ * 4 + q) * 128:(bq_ * 4 + q + 1) * 128]
                                         for q in range(4)]) for bq_ in range(4)]
                else:
                    groups = []
                    for par in range(2):
                        for bq_ in range(2):
                            srcs = [xn_s.ap()[:, (bq_ * 4 + q) * 256 + par:(bq_ * 4 + q) * 256 + 256:2]
                                    for q in range(4)]
                            groups.append((par * 8 + bq_ * 4, srcs))
                for gi, (kb, srcs) in enumerate(groups):
                    pt = PS[gi % 2]
                    w8("tensor", t_n)
                    w8("tensor", tk("sV"))
                    t_tr = None
                    for q4 in range(4):
                        t_tr = op("tensor", "sP" if q4 == 3 else None,
                                  lambda e, p=pt[:, q4 * 128:(q4 + 1) * 128], s_=srcs[q4]:
                                  e.transpose(p, s_, c_id[:, :]))
                    last_tr[0] = t_tr
                    w8("vector", t_tr)
                    hd = hi3[:, kb:kb + 4, tc * 128:(tc + 1) * 128]
                    ld_ = lo3[:, kb:kb + 4, tc * 128:(tc + 1) * 128]
                    op("vector", None, lambda e, p=pt, d=hd: e.tensor_copy(d, p[:, 0:512]))
                    t_last[0] = op("vector", "sV", lambda e, p=pt, d=ld_, h_=hd:
                                   e.scalar_tensor_tensor(d, p[:, 0:512], 1.0, h_, ALU.mult, ALU.subtract))

            if x_dram is not None and use_Ax:
                w8("sync", tk("sV"))
                w8("sync", tk("sA"))
            if x_dram is None:
                w8("scalar", tk("sV"))
                w8("vector", tk("sA"))
            for tc in range(NCHK):
                bi = tc % 2
                if use_Ax:
                    xt = A_x[:, tc * D:(tc + 1) * D]
                else:
                    xt = A_x[:, bi * D:(bi + 1) * D]
                    w8("sync", last_nrm[bi])
                t_x = None
                if x_dram is not None:
                    t_x = dma("sync", "sW", xt, x_dram[tc * 128:(tc + 1) * 128, :])
                sumc, sqc, muc = stat[:, 0:1], stat[:, 1:2], stat[:, 2:3]
                ex2c, mu2c, varc, rsc = stat[:, 3:4], stat[:, 4:5], stat[:, 5:6], stat[:, 6:7]
                w8("scalar", t_x)
                w8("scalar", last_tr[0])
                t_sq = op("scalar", "sA", lambda e, a=ksq[:, 0:1024], x_=xt[:, 0:1024], c=sqc:
                          e.activation(a, x_, ACT.Square, accum_out=c))
                t_sq = op("scalar", "sA", lambda e, a=ksq[:, 0:1024], x_=xt[:, 1024:2048], c=stat[:, 8:9]:
                          e.activation(a, x_, ACT.Square, accum_out=c))
                w8("vector", t_x)
                def vf(f):
                    w8("vector", op("vector", "sV", f))
                vf(lambda e, x_=xt, c=sumc: e.reduce_sum(c, x_, AXX))
                vf(lambda e, c=sumc, m=muc: e.tensor_scalar(m, c, 1.0 / D, None, ALU.mult))
                w8("vector", t_sq)
                vf(lambda e: e.tensor_add(stat[:, 9:10], sqc, stat[:, 8:9]))
                vf(lambda e, o=ex2c: e.tensor_scalar(o, stat[:, 9:10], 1.0 / D, None, ALU.mult))
                vf(lambda e, m=muc, o=mu2c: e.tensor_mul(o, m, m))
                t_var = op("vector", "sV", lambda e, a=ex2c, b=mu2c, o=varc: e.tensor_sub(o, a, b))
                w8("vector", t_var)
                w8("scalar", t_var)
                t_sd = op("scalar", "sA", lambda e, v=varc, o=stat[:, 7:8]:
                          e.activation(o, v, ACT.Sqrt, bias=1e-5))
                w8("vector", t_sd)
                vf(lambda e, o=rsc: e.reciprocal(o, stat[:, 7:8]))
                w8("vector", last_tr[0])
                op("vector", None, lambda e, x_=xt, o=xn_s[:, :], m=muc:
                   e.tensor_scalar(o, x_, m, None, ALU.subtract))
                t_n = op("vector", "sV", lambda e, o=xn_s[:, :], r=rsc:
                         e.tensor_scalar(o, o, r, None, ALU.mult))
                last_nrm[bi] = t_n
                do_transposes(t_n, bi, tc)
            return t_last[0]

        def emit_proj(psegs, wh_l, wl_l, xh_t, xl_t, nseg, want_sem):
            t_mm = None
            for seg in range(nseg):
                pp = psegs[seg]
                first = True
                for kc in range(16):
                    xh = xh_t[:, kc * TC + seg * 512: kc * TC + (seg + 1) * 512]
                    xl = xl_t[:, kc * TC + seg * 512: kc * TC + (seg + 1) * 512]
                    for pi in range(3):
                        a_ = wh_l[kc] if pi != 1 else wl_l[kc]
                        b_ = xh if pi != 2 else xl
                        last = (kc == 15 and pi == 2)
                        t_mm = op("tensor", want_sem if (last and seg == nseg - 1) else None,
                                  lambda e, p=pp, aa=a_, bb=b_, st=first, sp=last:
                                  e.matmul(p, aa, bb, start=st, stop=sp))
                        first = False
            return t_mm

        def emit_half(x_dram, full):
            t_xn = emit_ln_transposes(x_dram, full, permuted=False)

            t_wab = dma("sync", "sW", wbuf[0].ap()[:, 0:1024].rearrange("p (c f) -> p c f", f=64),
                        wab.rearrange("(c p) s -> p c s", p=128))
            w8("vector", t_wab)
            op("vector", None, lambda e: e.tensor_copy(wsp_hi[0][:, 0:1024], wbuf[0][:, 0:1024]))
            t_ws = op("vector", "sV", lambda e: e.scalar_tensor_tensor(
                wsp_lo[0][:, 0:1024], wbuf[0][:, 0:1024], 1.0, wsp_hi[0][:, 0:1024], ALU.mult, ALU.subtract))
            w8("tensor", t_ws)
            w8("tensor", t_xn)
            psegs = (PD[0][0:64, 0:512], PD[0][0:64, 512:1024])
            wha = [wsp_hi[0][:, kc * 64:(kc + 1) * 64] for kc in range(16)]
            wla = [wsp_lo[0][:, kc * 64:(kc + 1) * 64] for kc in range(16)]
            t_abmm = emit_proj(psegs, wha, wla, xnT_hi, xnT_lo, 2, "sP")
            w8("scalar", t_abmm)

            def af(f):
                w8("scalar", op("scalar", "sA", f))
            for seg in range(2):
                op("scalar", None, lambda e, s_=seg: e.activation(
                    ab[64:80, s_ * 512:(s_ + 1) * 512], PD[0][0:16, s_ * 512:(s_ + 1) * 512],
                    ACT.Identity, bias=c_bab[0:16, :]))
                af(lambda e, s_=seg: e.activation(
                    cdF[:, s_ * 512:(s_ + 1) * 512], PD[0][32:48, s_ * 512:(s_ + 1) * 512],
                    ACT.Identity, bias=c_bab[32:48, :]))
            af(lambda e: e.activation(ab[64:80, :], ab[64:80, :], ACT.Sigmoid))
            af(lambda e: e.activation(ab[0:16, :], ab[64:80, :], ACT.Ln, bias=1e-6))
            af(lambda e: e.activation(cdF[:, :], cdF[:, :], ACT.Exp))
            t_sp = op("scalar", "sA", lambda e: e.activation(ab[32:48, :], cdF[:, :], ACT.Ln, bias=1.0))
            w8("vector", t_sp)
            t_scan = None
            for c in range(NCHK):
                t_scan = op("vector", "sV", lambda e, c_=c: e.tensor_tensor_scan(
                    ab[0:16, c_ * 128:(c_ + 1) * 128], ab[0:16, c_ * 128:(c_ + 1) * 128],
                    c_zz[0:16, :], 0.0, ALU.add, ALU.add))
            w8("scalar", t_scan)
            t_cdF = op("scalar", "sA", lambda e: e.activation(cdF[:, :], ab[0:16, :], ACT.Exp))
            t_prep = None
            t_ce_prev = None
            for c in range(NCHK):
                w8("tensor", t_scan)
                w8("tensor", tk("sV"))
                t_tr0 = op("tensor", "sP", lambda e, c_=c: e.transpose(
                    PS[1][:, 0:16], ab[0:16, c_ * 128:(c_ + 1) * 128], c_id[0:16, 0:16]))
                t_tr = op("tensor", "sP", lambda e, c_=c: e.transpose(
                    PS[1][:, 16:32], ab[32:48, c_ * 128:(c_ + 1) * 128], c_id[32:48, 32:48]))
                w8("tensor", t_ce_prev)
                t_ce = op("tensor", "sP", lambda e, c_=c: e.transpose(
                    PS[2][0:1, 0:16], ab[0:16, c_ * 128 + 127:c_ * 128 + 128], c_id[0:16, 0:16]))
                w8("vector", t_tr)
                op("vector", None, lambda e, c_=c: e.tensor_copy(abT[:, c_ * 32:(c_ + 1) * 32], PS[1][:, 0:32]))
                w8("vector", t_ce)
                t_cer = op("vector", "sV", lambda e, c_=c: e.tensor_copy(
                    ce_row[:, c_ * 16:(c_ + 1) * 16], PS[2][0:1, 0:16]))
                t_ce_prev = t_cer
                w8("tensor", t_cer)
                t_bc = op("tensor", "sP", lambda e, c_=c: e.matmul(
                    PS[3][:, 0:16], c_or[:, :], ce_row[:, c_ * 16:(c_ + 1) * 16], start=True, stop=True))
                w8("vector", t_bc)
                w8("vector", t_cer)
                t_dp = op("vector", "sV", lambda e, c_=c: e.scalar_tensor_tensor(
                    dteT[:, c_ * 16:(c_ + 1) * 16], abT[:, c_ * 32:c_ * 32 + 16], -1.0,
                    PS[3][:, 0:16], ALU.mult, ALU.add))
                w8("scalar", t_dp)
                op("scalar", None, lambda e, c_=c: e.activation(
                    dteT[:, c_ * 16:(c_ + 1) * 16], dteT[:, c_ * 16:(c_ + 1) * 16], ACT.Exp))
                t_td = op("scalar", "sA", lambda e, c_=c: e.activation(
                    tdT[:, c_ * 16:(c_ + 1) * 16], PS[3][:, 0:16], ACT.Exp))
                w8("vector", t_td)
                t_prep = op("vector", "sV", lambda e, c_=c: e.tensor_mul(
                    dteT[:, c_ * 16:(c_ + 1) * 16], abT[:, c_ * 32 + 16:c_ * 32 + 32],
                    dteT[:, c_ * 16:(c_ + 1) * 16]))

            mats = ["k", "v"] + (["q"] if full else [])
            wsrc = {"k": wk, "v": wv, "q": wq}
            bsrc = {"k": bk, "v": bv, "q": bq}
            moff = {"k": 0, "v": 1, "q": 2}
            outsb = {"k": kT, "v": vT, "q": qT}
            nmat = len(mats)
            t_wld = {}
            t_wsp_used = [t_ws, t_ws]
            t_wbuf_used = [t_ws]

            def load_w(h, m):
                w8("sync", t_wbuf_used[0])
                t_wld[(h, m)] = dma("sync", "sWl",
                                    wbuf[0].ap().rearrange("p (c f) -> p c f", f=128),
                                    wsrc[m][:, h * 128:(h + 1) * 128].rearrange("(c p) s -> p c s", p=128))

            order = [(h, m) for h in range(NH) for m in mats]
            load_w(*order[0])
            t_pev = {}
            t_head_mm = {}
            for oi, (h, m) in enumerate(order):
                slot = oi % 2
                mo = moff[m]
                w8("vector", t_wld[(h, m)])
                w8("vector", t_wsp_used[slot])
                op("vector", None, lambda e, b=wbuf[0], s_=slot: e.tensor_copy(
                    wsp_hi[s_][:, :], b[:, :]))
                t_ws2 = op("vector", "sV", lambda e, b=wbuf[0], s_=slot: e.scalar_tensor_tensor(
                    wsp_lo[s_][:, :], b[:, :], 1.0, wsp_hi[s_][:, :], ALU.mult, ALU.subtract))
                t_wbuf_used[0] = t_ws2
                if oi + 1 < len(order):
                    load_w(*order[oi + 1])
                if m == "k":
                    w8("sync", tk("sA"))
                    for m2 in mats:
                        dma("sync", "sW", bcol[:, moff[m2]:moff[m2] + 1],
                            bsrc[m2][h * 128:(h + 1) * 128, :])
                t_b = tk("sW")
                w8("tensor", t_ws2)
                w8("tensor", tk("sA"))
                psegs = (PD[0][:, 0:512], PD[0][:, 512:1024])
                wha = [wsp_hi[slot][:, kc * 128:(kc + 1) * 128] for kc in range(16)]
                wla = [wsp_lo[slot][:, kc * 128:(kc + 1) * 128] for kc in range(16)]
                t_mm = emit_proj(psegs, wha, wla, xnT_hi, xnT_lo, 2, "sP")
                t_wsp_used[slot] = t_mm
                w8("scalar", t_mm)
                w8("scalar", t_b)
                for seg in range(2):
                    t_pev[m] = op("scalar", "sA", lambda e, s_=seg, o=outsb[m], mo_=mo:
                                  e.activation(o[:, s_ * 512:(s_ + 1) * 512],
                                               PD[0][:, s_ * 512:(s_ + 1) * 512],
                                               ACT.Identity, bias=bcol[:, mo_:mo_ + 1]))
                if m != mats[-1]:
                    continue
                if full:
                    w8("sync", t_head_mm.get(h - 1))
                    dma("sync", "sW", c_sel.ap(), sel16[:, h * 128:(h + 1) * 128])
                    t_selh = dma("sync", "sW", c_nsl.ap(), nsel16[:, h * 128:(h + 1) * 128])
                w8("scalar", t_pev["k"])
                t_ksq = op("scalar", "sA", lambda e: e.activation(ksq[:, :], kT[:, :], ACT.Square))
                t_rq = None
                for seg in range(2):
                    w8("tensor", t_ksq)
                    w8("tensor", t_rq)
                    t_rm = op("tensor", "sP", lambda e, s_=seg: e.matmul(
                        PS[2][0:1, 0:512], c_oc[:, :], ksq[:, s_ * 512:(s_ + 1) * 512],
                        start=True, stop=True))
                    w8("scalar", t_rm)
                    t_rq = op("scalar", "sA", lambda e, s_=seg: e.activation(
                        rn_row[:, s_ * 512:(s_ + 1) * 512], PS[2][0:1, 0:512], ACT.Sqrt))
                w8("vector", t_rq)
                t_mx = op("vector", "sV", lambda e: e.tensor_scalar(
                    rn_row[:, :], rn_row[:, :], 1e-12, None, ALU.max))
                w8("vector", t_mx)
                t_rn = op("vector", "sV", lambda e: e.reciprocal(rn_row[:, :], rn_row[:, :]))
                t_rc = None
                for c in range(NCHK):
                    w8("tensor", t_rn)
                    w8("tensor", t_rc)
                    t_rt = op("tensor", "sP", lambda e, c_=c: e.transpose(
                        PS[1][:, 0:1], rn_row[:, c_ * 128:(c_ + 1) * 128], c_id[0:1, 0:1]))
                    w8("vector", t_rt)
                    t_rc = op("vector", "sV", lambda e, c_=c: e.tensor_copy(
                        rn_cols[:, c_:c_ + 1], PS[1][:, 0:1]))
                t_state_up = None
                for c in range(NCHK):
                    cs0, cs1 = c * 128, (c + 1) * 128
                    w8("tensor", t_pev["v"])
                    w8("tensor", tk("sV"))
                    t_kt = op("tensor", "sP", lambda e, a=cs0, b=cs1: e.transpose(
                        PD[0][:, 0:128], kT[:, a:b], c_id[:, :]))
                    t_vt = op("tensor", "sP", lambda e, a=cs0, b=cs1: e.transpose(
                        PD[0][:, 512:640], vT[:, a:b], c_id[:, :]))
                    w8("vector", t_kt)
                    w8("vector", t_rc)
                    op("vector", None, lambda e, c_=c: e.tensor_scalar(
                        k_tok[:, c_ * 128:(c_ + 1) * 128], PD[0][:, 0:128],
                        rn_cols[:, c_:c_ + 1], None, ALU.mult))
                    w8("vector", t_vt)
                    t_vc = op("vector", "sV", lambda e, c_=c: e.tensor_copy(
                        v_tok[:, c_ * 128:(c_ + 1) * 128], PD[0][:, 512:640]))
                    w8("vector", t_prep)
                    t_brn = op("vector", "sV", lambda e, c_=c, h_=h: e.tensor_mul(
                        brn_c[:, 0:1], abT[:, c_ * 32 + 16 + h_:c_ * 32 + 17 + h_],
                        rn_cols[:, c_:c_ + 1]))
                    w8("vector", t_brn)
                    t_kw = op("vector", "sV", lambda e, c_=c, h_=h: e.tensor_scalar(
                        kwt[:, :], k_tok[:, c_ * 128:(c_ + 1) * 128],
                        sdT[:, c_ * 16 + h_:c_ * 16 + h_ + 1], None, ALU.mult))
                    if full:
                        w8("tensor", t_pev["q"])
                        w8("tensor", t_selh)
                        w8("tensor", t_vc)
                        t_sc_mm = op("tensor", "sP", lambda e, a=cs0, b=cs1: e.matmul(
                            PS[0][:, 0:128], kT[:, a:b], qT[:, a:b], start=True, stop=True))
                        op("tensor", None, lambda e, c_=c, h_=h: e.matmul(
                            PS[1][:, 0:128], c_sel[:, :],
                            ab[0:16, c_ * 128:(c_ + 1) * 128], start=True, stop=False))
                        t_db = op("tensor", "sP", lambda e, c_=c, h_=h: e.matmul(
                            PS[1][:, 0:128], ab[0:16, c_ * 128:(c_ + 1) * 128],
                            c_nsl[:, :], start=False, stop=True))
                        w8("tensor", t_cdF)
                        t_cb = op("tensor", "sP", lambda e, c_=c, h_=h: e.matmul(
                            PS[2][:, 0:128], c_sel[:, :],
                            cdF[0:16, c_ * 128:(c_ + 1) * 128], start=True, stop=True))
                        w8("vector", t_db)
                        t_dm = op("vector", "sV", lambda e: e.scalar_tensor_tensor(
                            dmt[:, :], PS[1][:, 0:128], 1.0, c_mask[:, :], ALU.mult, ALU.add))
                        w8("scalar", t_dm)
                        t_dk = op("scalar", "sA", lambda e: e.activation(dket[:, :], dmt[:, :], ACT.Exp))
                        w8("vector", t_sc_mm)
                        w8("vector", t_dk)
                        t_at = op("vector", "sV", lambda e: e.scalar_tensor_tensor(
                            attnT[:, :], PS[0][:, 0:128], brn_c[:, 0:1], dket[:, :],
                            ALU.mult, ALU.mult))
                        w8("vector", t_cb)
                        t_qc = op("vector", "sV", lambda e, a=cs0, b=cs1: e.tensor_tensor(
                            q_cd[:, :], qT[:, a:b], PS[2][:, 0:128], ALU.mult))
                        w8("tensor", t_at)
                        w8("tensor", t_qc)
                        w8("tensor", t_state_up)
                        op("tensor", None, lambda e, a=cs0, b=cs1: e.matmul(
                            PS[3][:, 0:128], v_tok[:, a:b], attnT[:, :], start=True, stop=False))
                        t_ym = op("tensor", "sP", lambda e, h_=h: e.matmul(
                            PS[3][:, 0:128], state[:, h_ * 128:(h_ + 1) * 128], q_cd[:, :],
                            start=False, stop=True))
                        w8("vector", t_ym)
                        t_ye = op("vector", "sV", lambda e: e.tensor_copy(y_sb[:, :], PS[3][:, 0:128]))
                        if dbg_ch is not None and dbg_ch == (c, h):
                            srcsel = {"y": y_sb[:, :], "q_cd": q_cd[:, :], "attnT": attnT[:, :],
                                      "state": state[:, h * 128:(h + 1) * 128],
                                      "v": v_tok[:, cs0:cs1], "kw": kwt[:, :],
                                      "scores": PS[0][:, 0:128], "brn": brn_c[:, 0:1],
                                      "dk": dket[:, :]}[dbg_what] if dbg_what != "rncols" else None
                            if dbg_what == "rncols":
                                op("vector", "sV", lambda e: e.tensor_copy(
                                    dbg_grab[:, 0:NCHK], rn_cols[:, :]))
                            else:
                                op("vector", "sV", lambda e, s_=srcsel: e.tensor_copy(dbg_grab[:, :], s_))
                        w8("tensor", t_ye)
                        t_yt = op("tensor", "sP", lambda e: e.transpose(
                            PD[1][:, 0:128], y_sb[:, :], c_id[:, :]))
                        w8("vector", t_yt)
                        op("vector", "sV", lambda e, c_=c, h_=h: e.tensor_tensor(
                            A_x[:, c_ * D + h_ * 128: c_ * D + (h_ + 1) * 128],
                            A_x[:, c_ * D + h_ * 128: c_ * D + (h_ + 1) * 128],
                            PD[1][:, 0:128], ALU.add))
                    w8("tensor", t_kw)
                    w8("tensor", t_state_up)
                    t_dl = op("tensor", "sP", lambda e, a=cs0, b=cs1: e.matmul(
                        PD[1][:, 512:640], kwt[:, :], v_tok[:, a:b], start=True, stop=True))
                    w8("vector", t_dl)
                    t_state_up = op("vector", "sV", lambda e, c_=c, h_=h: e.scalar_tensor_tensor(
                        state[:, h_ * 128:(h_ + 1) * 128], state[:, h_ * 128:(h_ + 1) * 128],
                        tdT[:, c_ * 16 + h_:c_ * 16 + h_ + 1], PD[1][:, 512:640],
                        ALU.mult, ALU.add))
                t_head_mm[h] = tk("sP")
                if not full:
                    w8("vector", t_const)
                    w8("vector", t_state_up)
                    op("vector", "sV", lambda e, h_=h: e.tensor_scalar(
                        state[:, h_ * 128:(h_ + 1) * 128], state[:, h_ * 128:(h_ + 1) * 128],
                        c_flag[:, 0:1], None, ALU.mult))

        emit_half(xF, False)
        if stop_after >= 1:
            emit_half(xO, True)

        if stop_after >= 2:
            emit_ln_transposes(None, True, permuted=True)

        for (tname, tshape, tdt) in taps:
            if tname in ("kT", "vT", "qT", "k_tok", "v_tok", "cla", "sp", "sig", "dbg_grab", "state", "cdF",
                         "dteT", "tdT", "sdT", "abT", "attnT", "rn_row", "A_x",
                         "xnT_hi", "xnT_lo"):
                srcs = dict(kT=kT, vT=vT, qT=qT, k_tok=k_tok, v_tok=v_tok,
                            dbg_grab=c_mask,
                            cla=ab[0:16, :], sp=ab[32:48, :], sig=ab[64:80, :],
                            state=state, cdF=cdF, dteT=dteT, tdT=tdT, sdT=sdT, abT=abT,
                            attnT=attnT, rn_row=rn_row)
                if tname == "A_x":
                    src_ap = A_x[:, 0:tshape[1]]
                elif tname in ("xnT_hi", "xnT_lo"):
                    src_ap = (xnT_hi if tname == "xnT_hi" else xnT_lo)[:, 0:tshape[1]]
                else:
                    src_ap = srcs[tname][0:tshape[0], 0:tshape[1]]
                w8("sync", tk("sV"))
                w8("sync", tk("sP"))
                w8("sync", tk("sA"))
                dma("sync", "sO", tap_dram[tname], src_ap)
                tap_sbuf[tname] = True

        astk.close()

        t_flush_last = {}
        if stop_after >= 3:
            fstk = ctx.enter_context(contextlib.ExitStack())
            it_hi = sb("it_hi", [128, 4 * TC], FP16, fstk)
            it_lo = sb("it_lo", [128, 4 * TC], FP16, fstk)
            it32 = sb("it32", [128, TC], F32, fstk)
            w1g = sb("w1g", [128, 2048], FP16, fstk)
            w2g = sb("w2g", [128, 2048], FP16, fstk)
            w1u = sb("w1u", [128, 2048], FP16, fstk)
            w2u = sb("w2u", [128, 2048], FP16, fstk)
            w1d = [[sb(f"w1d{i}_{p}", [128, 512], FP16, fstk) for p in range(2)] for i in range(4)]
            w2d = [[sb(f"w2d{i}_{p}", [128, 512], FP16, fstk) for p in range(2)] for i in range(4)]
            pkb = [sb(f"pkb{i}", [128, 1024], I32, fstk) for i in range(2)]
            srb = [sb(f"srb{i}", [128, 1024], F32, fstk) for i in range(2)]
            pkd = sb("pkd", [128, 512], I32, fstk)
            srd = sb("srd", [128, 512], F32, fstk)
            DQW = 512
            shi = sb("shi", [128, DQW], BF16, fstk)
            slo = sb("slo", [128, DQW], BF16, fstk)
            dq_c = sb("dq_c", [128, DQW], I32, fstk)
            dq_s = sb("dq_s", [128, DQW], I32, fstk)
            dq_t1 = sb("dq_t1", [128, DQW], BF16, fstk)

            t_attn_done_v = tk("sV")
            t_attn_done_p = tk("sP")
            t_attn_done_a = tk("sA")
            for eng in ("gpsimd",):
                w8(eng, t_attn_done_v)
                w8(eng, t_attn_done_p)
                w8(eng, t_attn_done_a)

            def emit_dq_plane(pk_ap, sr_ap, plane, w1_ap, w2_ap, do_scale, t_sr=None,
                              t_shi_free=None):
                W = DQW
                if plane == 0:
                    op("vector", None, lambda e: e.tensor_scalar(dq_c[:, 0:W], pk_ap, 7, None,
                                                                 ALU.bitwise_and))
                    op("vector", None, lambda e: e.tensor_scalar(dq_s[:, 0:W], pk_ap, 8, 2,
                                                                 ALU.bitwise_and, ALU.logical_shift_right))
                else:
                    op("vector", None, lambda e: e.tensor_scalar(dq_c[:, 0:W], pk_ap, 4, 7,
                                                                 ALU.logical_shift_right, ALU.bitwise_and))
                    op("vector", None, lambda e: e.tensor_scalar(dq_s[:, 0:W], pk_ap, 128, 6,
                                                                 ALU.bitwise_and, ALU.logical_shift_right))
                op("vector", None, lambda e: e.tensor_scalar(
                    dq_t1[:, 0:W], dq_c[:, 0:W], -4.0, 2.0, ALU.add, ALU.mult))
                op("vector", None, lambda e: e.scalar_tensor_tensor(
                    dq_t1[:, 0:W], dq_c[:, 0:W], -2.0, dq_t1[:, 0:W], ALU.add, ALU.max))
                op("vector", None, lambda e: e.scalar_tensor_tensor(
                    dq_t1[:, 0:W], dq_c[:, 0:W], 0.5, dq_t1[:, 0:W], ALU.mult, ALU.max))
                op("vector", None, lambda e: e.scalar_tensor_tensor(
                    dq_t1[:, 0:W], dq_s[:, 0:W], -1.0, dq_t1[:, 0:W], ALU.add, ALU.mult))
                if do_scale:
                    w8("vector", t_sr)
                    op("vector", None, lambda e: e.tensor_scalar(shi[:, 0:W], sr_ap, -1.0, None, ALU.mult))
                    op("vector", None, lambda e: e.scalar_tensor_tensor(slo[:, 0:W], sr_ap, -1.0,
                                                                        shi[:, 0:W], ALU.mult, ALU.subtract))
                op("vector", None, lambda e: e.tensor_tensor(w1_ap, dq_t1[:, 0:W], shi[:, 0:W], ALU.mult))
                return op("vector", "sV", lambda e: e.tensor_tensor(w2_ap, dq_t1[:, 0:W], slo[:, 0:W], ALU.mult))

            t_pk_prev = [None, None]
            t_dq_last = [None]

            def emit_gu_nchunk(mat_pk, mat_sc, nn, bufi, w1b, w2b, t_wfree):
                n0 = nn * 128
                w8("gpsimd", t_pk_prev[bufi])
                strm = "sGt" if bufi == 0 else "sU"
                t_pk = dma("gpsimd", strm, pkb[bufi].ap().rearrange("p (c f) -> p c f", f=128),
                           mat_pk[:, n0:n0 + 128].rearrange("(c p) s -> p c s", p=128))
                t_sr = dma("gpsimd", strm, srb[bufi].ap().rearrange("p (c f) -> p c f", f=128),
                           mat_sc[:, n0:n0 + 128].rearrange("(c p) s -> p c s", p=128))
                w8("vector", t_pk)
                w8("vector", t_sr)
                w8("vector", t_wfree)
                t_dq = t_dq_last[0]
                for s0 in range(0, 1024, DQW):
                    for plane in (0, 1):
                        t_dq = emit_dq_plane(
                            pkb[bufi].ap()[:, s0:s0 + DQW], srb[bufi].ap()[:, s0:s0 + DQW], plane,
                            w1b.ap()[:, plane * 1024 + s0: plane * 1024 + s0 + DQW],
                            w2b.ap()[:, plane * 1024 + s0: plane * 1024 + s0 + DQW],
                            plane == 0, t_sr=t_sr, t_shi_free=t_dq)
                t_dq_last[0] = t_dq
                t_pk_prev[bufi] = t_dq
                return t_dq

            t_pkd_prev = [None]
            t_dn_used = {}
            t_dq_tile = {}

            slot_pk = [pkd.ap()[:, 0:512], pkb[1].ap()[:, 0:512],
                       pkb[1].ap()[:, 512:1024], pkb[0].ap()[:, 0:512]]
            slot_sr = [srd.ap()[:, 0:512], srb[1].ap()[:, 0:512],
                       srb[1].ap()[:, 512:1024], srb[0].ap()[:, 0:512]]
            slot_sem = ["sD0", "sD1", "sD2", "sD3"]
            t_slot_dq = [None, None, None, None]

            def emit_down_tile(mc, hq, mci, par):
                plane = 0 if mc < 64 else 1
                jc = mc if plane == 0 else mc - 64
                h0 = hq * 512
                s = mci
                w8("gpsimd", t_slot_dq[s])
                if s in (1, 2):
                    w8("gpsimd", t_pk_prev[1])
                if s == 3:
                    w8("gpsimd", t_pk_prev[0])
                t_pk = dma("gpsimd", slot_sem[s], slot_pk[s],
                           dpk[jc * 128:(jc + 1) * 128, h0:h0 + 512])
                t_sr = dma("gpsimd", slot_sem[s], slot_sr[s],
                           dsc[jc * 128:(jc + 1) * 128, h0:h0 + 512])
                w8("vector", t_sr)
                w8("vector", t_dn_used.get((mci, par)))
                t_dq = emit_dq_plane(slot_pk[s], slot_sr[s], plane,
                                     w1d[mci][par].ap()[:, 0:512],
                                     w2d[mci][par].ap()[:, 0:512], True,
                                     t_sr=t_sr, t_shi_free=t_dq_last[0])
                t_dq_last[0] = t_dq
                t_slot_dq[s] = t_dq
                if s in (1, 2):
                    t_pk_prev[1] = t_dq
                if s == 3:
                    t_pk_prev[0] = t_dq
                t_dq_tile[(mci, par)] = t_dq

            NGRP = 32
            t_gu_used = {"g": None, "u": None}
            t_dq_g = emit_gu_nchunk(gpk, gsc, 0, 0, w1g, w2g, None)
            t_dq_u = None
            t_pd_reader = [t_attn_done_v, t_attn_done_v]
            t_it32_add = [None, None]
            for g in range(NGRP):
                for nli in range(4):
                    nn = g * 4 + nli
                    t_dq_u = emit_gu_nchunk(upk, usc, nn, 1, w1u, w2u, t_gu_used["u"])
                    pg = PD[nli % 2]
                    pu = PD[1 - nli % 2]
                    w8("tensor", t_dq_g)
                    w8("tensor", t_pd_reader[nli % 2])
                    psegs = (pg[:, 0:512], pg[:, 512:1024])
                    wha = [w1g[:, kc * 128:(kc + 1) * 128] for kc in range(16)]
                    wla = [w2g[:, kc * 128:(kc + 1) * 128] for kc in range(16)]
                    t_mm_g = emit_proj(psegs, wha, wla, xnT_hi, xnT_lo, 2, "sP")
                    t_gu_used["g"] = t_mm_g
                    w8("scalar", t_mm_g)
                    w8("scalar", t_it32_add[0])
                    w8("scalar", t_it32_add[1])
                    t_gl = op("scalar", "sA", lambda e, p=pg: e.activation(
                        it32[:, :], p[:, 0:1024], ACT.Gelu_apprx_tanh))
                    t_pd_reader[nli % 2] = t_gl
                    if nn + 1 < MLP // 128:
                        t_dq_g = emit_gu_nchunk(gpk, gsc, nn + 1, 0, w1g, w2g, t_gu_used["g"])
                    w8("tensor", t_dq_u)
                    w8("tensor", t_pd_reader[1 - nli % 2])
                    psegs = (pu[:, 0:512], pu[:, 512:1024])
                    wha = [w1u[:, kc * 128:(kc + 1) * 128] for kc in range(16)]
                    wla = [w2u[:, kc * 128:(kc + 1) * 128] for kc in range(16)]
                    t_mm_u = emit_proj(psegs, wha, wla, xnT_hi, xnT_lo, 2, "sP")
                    t_gu_used["u"] = t_mm_u
                    w8("vector", t_gl)
                    w8("vector", t_mm_u)
                    t_i32 = op("vector", "sV", lambda e, p=pu: e.tensor_tensor(
                        it32[:, :], it32[:, :], p[:, 0:1024], ALU.mult))
                    t_pd_reader[1 - nli % 2] = t_i32
                    w8("scalar", t_i32)
                    t_ih = op("scalar", "sA", lambda e, n_=nli: e.activation(
                        it_hi[:, n_ * TC:(n_ + 1) * TC], it32[:, :], ACT.Copy))
                    w8("vector", t_ih)
                    op("vector", "sV", lambda e, n_=nli: e.scalar_tensor_tensor(
                        it_lo[:, n_ * TC:(n_ + 1) * TC], it32[:, :], 1.0,
                        it_hi[:, n_ * TC:(n_ + 1) * TC], ALU.mult, ALU.subtract))
                t_inter = tk("sV")
                for hq in range(4):
                    d = g * 4 + hq
                    par = d % 2
                    if d == 0:
                        for mci in range(4):
                            emit_down_tile(mci, 0, mci, 0)
                    nxt = (g, hq + 1) if hq < 3 else ((g + 1, 0) if g + 1 < NGRP else None)
                    for tc in range(NCHK):
                        pdd = PD[tc % 2]
                        w8("tensor", t_pd_reader[tc % 2])
                        w8("tensor", t_inter)
                        t_mm_d = None
                        for mci in range(4):
                            w8("tensor", t_dq_tile[(mci, par)])
                            for pi in range(3):
                                wdd = (w1d[mci][par] if pi != 1 else w2d[mci][par]).ap()[:, 0:512]
                                itt = (it_hi if pi != 2 else it_lo).ap()[
                                    :, mci * TC + tc * 128: mci * TC + (tc + 1) * 128]
                                last = (mci == 3 and pi == 2)
                                t_mm_d = op("tensor", "sP" if last else None,
                                            lambda e, p=pdd[:, 0:512], w_=wdd,
                                            i_=itt, st=(mci == 0 and pi == 0), sp=last:
                                            e.matmul(p, i_, w_, start=st, stop=sp))
                        w8("scalar", t_mm_d)
                        w8("scalar", t_it32_add[tc % 2])
                        t_ev = op("scalar", "sA", lambda e, t_=tc, p=pdd: e.activation(
                            it32[:, (t_ % 2) * 512:(t_ % 2) * 512 + 512], p[:, 0:512], ACT.Copy))
                        w8("gpsimd", t_ev)
                        t_fl = op("gpsimd", "sGc", lambda e, t_=tc, h_=hq: e.tensor_tensor(
                            A_x[:, t_ * D + h_ * 512: t_ * D + (h_ + 1) * 512],
                            A_x[:, t_ * D + h_ * 512: t_ * D + (h_ + 1) * 512],
                            it32[:, (t_ % 2) * 512:(t_ % 2) * 512 + 512], ALU.add))
                        t_it32_add[tc % 2] = t_fl
                        t_pd_reader[tc % 2] = t_ev
                        t_flush_last[(hq % 2, tc)] = t_fl
                        if nxt is not None and 2 <= tc <= 5:
                            ng, nhq = nxt
                            emit_down_tile(ng * 4 + (tc - 2), nhq, tc - 2, 1 - par)
                    t_dn_new = tk("sP")
                    for mci in range(4):
                        t_dn_used[(mci, par)] = t_dn_new
            for (tname, tshape, tdt) in taps:
                if tname in tap_sbuf:
                    continue
                srcs = dict(it_hi=it_hi, it_lo=it_lo, w1g=w1g, w2g=w2g, w1d0=w1d[0][0])
                if tname not in srcs:
                    continue
                w8("sync", tk("sV"))
                w8("sync", tk("sP"))
                dma("sync", "sO", tap_dram[tname], srcs[tname][0:tshape[0], 0:tshape[1]])
                tap_sbuf[tname] = True

        for tc in range(NCHK):
            if stop_after >= 3:
                w8("sync", t_flush_last[(0, tc)])
                w8("sync", t_flush_last[(1, tc)])
            else:
                w8("sync", tk("sV"))
                w8("sync", tk("sP"))
                w8("sync", tk("sA"))
            dma("sync", "sO", out[tc * 128:(tc + 1) * 128, :], A_x[:, tc * D:(tc + 1) * D])

        em("gpsimd", lambda e: e.wait_ge(SEMH["sO"], cnt["sO"]))

        block = ctx.enter_context(nc.Block())

        def runner(name):
            def run(e):
                for fn in Q[name]:
                    fn(e)
            return run

        block.sync(runner("sync"))
        block.tensor(runner("tensor"))
        block.vector(runner("vector"))
        block.scalar(runner("scalar"))
        block.gpsimd(runner("gpsimd"))

    return nc


_SEL16 = np.ascontiguousarray(
    (np.arange(16)[:, None, None] == np.arange(16)[None, :, None])
    .astype(np.float32).repeat(128, axis=2).reshape(16, 16 * 128))


def _host_prep(inputs):
    x = np.asarray(inputs["L0_attn_x"], dtype=np.float32)
    perm = np.concatenate([np.arange(0, MLP, 2), np.arange(1, MLP, 2)])
    rep16 = np.repeat(np.arange(64), 16)
    rep16d = np.repeat(np.arange(512), 16)
    gpkT = np.ascontiguousarray(np.asarray(inputs["L0_ffn_gate_W_packed"], np.int32).T[:, perm])
    upkT = np.ascontiguousarray(np.asarray(inputs["L0_ffn_up_W_packed"], np.int32).T[:, perm])
    dpkT = np.ascontiguousarray(np.asarray(inputs["L0_ffn_down_W_packed"], np.int32).T)
    gscT = np.ascontiguousarray(np.asarray(inputs["L0_ffn_gate_scales"], np.float32).T[rep16][:, perm])
    uscT = np.ascontiguousarray(np.asarray(inputs["L0_ffn_up_scales"], np.float32).T[rep16][:, perm])
    dscT = np.ascontiguousarray(np.asarray(inputs["L0_ffn_down_scales"], np.float32).T[rep16d])
    _z16 = np.zeros((D, 16), np.float32)
    wabf = np.ascontiguousarray(np.concatenate(
        [np.asarray(inputs["L0_attn_Walpha"], np.float32), _z16,
         np.asarray(inputs["L0_attn_Wbeta"], np.float32), _z16], axis=1))
    babf = np.ascontiguousarray(np.concatenate(
        [np.asarray(inputs["L0_attn_balpha"], np.float32), np.zeros(16, np.float32),
         np.asarray(inputs["L0_attn_bbeta"], np.float32), np.zeros(16, np.float32)])).reshape(64, 1)
    e = np.arange(128)
    mask = np.where(e[None, :] >= e[:, None], 0.0, NEG).astype(np.float32)
    shared = dict(
        wq=np.ascontiguousarray(np.asarray(inputs["L0_attn_Wq"], np.float32)),
        wk=np.ascontiguousarray(np.asarray(inputs["L0_attn_Wk"], np.float32)),
        wv=np.ascontiguousarray(np.asarray(inputs["L0_attn_Wv"], np.float32)),
        bq=np.asarray(inputs["L0_attn_bq"], np.float32).reshape(D, 1).copy(),
        bk=np.asarray(inputs["L0_attn_bk"], np.float32).reshape(D, 1).copy(),
        bv=np.asarray(inputs["L0_attn_bv"], np.float32).reshape(D, 1).copy(),
        wab=wabf, bab=babf,
        gpk=gpkT, upk=upkT, dpk=dpkT, gsc=gscT, usc=uscT, dsc=dscT,
        ident=np.eye(128, dtype=np.float32),
        ones_row=np.ones((1, 128), np.float32),
        ones_col=np.ones((128, 1), np.float32),
        maskneg=mask,
        sel16=_SEL16, nsel16=np.ascontiguousarray(-_SEL16),
        zz128=np.zeros((128, 128), np.float32),
        constv=np.concatenate([np.full((128, 1), 1e-5, np.float32),
                               np.full((128, 1), 1e-6, np.float32),
                               np.full((128, 1), 1.0, np.float32)], axis=1),
    )
    in_maps = []
    for c in range(8):
        b, own = c // 2, c % 2
        m = dict(shared)
        m["xO"] = np.ascontiguousarray(x[b, own * TC:(own + 1) * TC, :])
        m["xF"] = np.ascontiguousarray(x[b, 0:TC, :])
        m["flagF"] = np.full((128, 1), 1.0 if own == 1 else 0.0, np.float32)
        in_maps.append(m)
    return in_maps


def kernel(**inputs):
    in_maps = _host_prep(inputs)
    nc = build_program()
    res = run_bass_kernel_spmd(nc, in_maps, core_ids=list(range(8)))
    outp = np.empty((4, 2048, D), dtype=np.float32)
    for c in range(8):
        b, own = c // 2, c % 2
        outp[b, own * TC:(own + 1) * TC, :] = np.asarray(res.results[c]["out"])
    return outp
